# version 21
# baseline (speedup 1.0000x reference)
import numpy as np
from contextlib import ExitStack

import concourse.bass as bass
import concourse.mybir as mybir
from concourse.bass_utils import run_bass_kernel_spmd

F32 = mybir.dt.float32
BF16 = mybir.dt.bfloat16
AF = mybir.ActivationFunctionType
ALU = mybir.AluOpType
AX = mybir.AxisListType.X

NCORES = 8
TC = 2048
CH = 128
NCH = TC // CH
TGT = 512
NTG = TC // TGT
D = 1024
ND = D // 128
H = 8
NE = 16
MIDR = 256
NSH = 2
MIDS = 512
NMID = NSH * MIDS + NE * MIDR
NFT = NMID // 128
FMLP = 4096
EPS = 1e-5

ENGINES = ("sync", "tensor", "vector", "scalar", "gpsimd")


class Sem:
    __slots__ = ("h", "n")

    def __init__(self, h):
        self.h = h
        self.n = 0


class Em:
    def __init__(self, nc):
        self.nc = nc
        self.q = {k: [] for k in ENGINES}
        self.lastwait = {}
        self.sems = []
        self.autosem = {}

    def wait(self, eng, sem, val=None):
        if val is None:
            val = sem.n
        if val <= 0:
            return
        key = (eng, id(sem))
        if self.lastwait.get(key, -1) >= val:
            return
        self.lastwait[key] = val
        self.q[eng].append(lambda e, h=sem.h, v=val: e.wait_ge(h, v))

    def join(self, engs=ENGINES):
        for eng in engs:
            for s in self.sems:
                self.wait(eng, s)

    def op(self, eng, fn, inc=None, inck=1):
        auto = self.autosem.get(eng)
        if auto is not None:
            self.wait(eng, auto)
            inc, inck = auto, 1
        if inc is not None:
            inc.n += inck
            self.q[eng].append(lambda e, f=fn, h=inc.h, k=inck: f(e).then_inc(h, k))
        else:
            self.q[eng].append(lambda e, f=fn: f(e))

    def flush(self):
        if not any(self.q.values()):
            return
        with self.nc.Block() as blk:
            for name in ENGINES:
                fns = self.q[name]
                if not fns:
                    continue

                def body(e, fns=fns):
                    for f in fns:
                        f(e)
                getattr(blk, name)(body)
        self.q = {k: [] for k in ENGINES}
        self.lastwait = {}


def build_nc():
    nc = bass.Bass("TRN2", target_bir_lowering=False, debug=False)
    dram = {}

    def din(name, shape, dt=F32):
        dram[name] = nc.dram_tensor(name, list(shape), dt, kind="ExternalInput").ap()
        return dram[name]

    def dout(name, shape, dt=F32):
        dram[name] = nc.dram_tensor(name, list(shape), dt, kind="ExternalOutput").ap()
        return dram[name]

    def dscr(name, shape, dt):
        dram[name] = nc.dram_tensor(name, list(shape), dt, kind="Internal").ap()
        return dram[name]

    x_in = din("x", [TC, D])
    for nm, sh in [("wq", [D, D]), ("wk", [D, D]), ("wv", [D, D]), ("wab", [D, 16]),
                   ("wo", [D, D]), ("w1", [D, FMLP]), ("w2", [FMLP, D]),
                   ("rw", [D, NE]), ("wgu", [D, 2 * NMID]), ("wd", [NMID, D]),
                   ("ln1w", [128, ND]), ("ln1b", [128, ND]), ("ln2w", [128, ND]),
                   ("ln2b", [128, ND]), ("b1", [128, FMLP // 128]), ("b2", [128, ND]),
                   ("identf", [128, 128]), ("utri", [128, 128]), ("lastrow", [128, 128]),
                   ("sel16", [16, 16 * 128]), ("ltri_sneg", [128, H * 128]),
                   ("utri_sneg", [128, H * 128]), ("utri_incl", [128, H * 128]),
                   ("onesf_row", [1, 128]), ("onesf_col", [128, 1]),
                   ("coresel", [128, NCORES])]:
        din(nm, sh)
    din("identb", [128, 128], BF16)
    din("identb_rep", [128, H * 128], BF16)
    y = dout("y", [TC, D])

    xn_hbm = dscr("xn_hbm", [NCH, 128, D], F32)
    base_hbm = dscr("base_hbm", [NCH, 128, D], F32)
    obase_hbm = dscr("obase_hbm", [NCH, 128, H * 128], BF16)
    zt_hbm = dscr("zt_hbm", [NCH, 128, H * 128], BF16)
    pt_hbm = dscr("pt_hbm", [NCH, 128, H * 128], BF16)
    qa_hbm = dscr("qa_hbm", [NCH, 128, H * 128], BF16)
    acc_hbm = dscr("acc_hbm", [NTG, 128, ND * TGT], F32)
    wgu_t = dscr("wgu_t", [2 * NFT, 128, ND * 128], BF16)
    wd_t = dscr("wd_t", [NFT, 128, ND * 128], BF16)
    w1_t = dscr("w1_t", [FMLP // 128, 128, ND * 128], BF16)
    w2_t = dscr("w2_t", [FMLP // 128, 128, ND * 128], BF16)
    ag_in = dscr("ag_in", [128, H * 256], BF16)
    ag_out = dscr("ag_out", [NCORES * 128, H * 256], BF16)

    st = ExitStack()
    SB = lambda name, shape, dt: st.enter_context(nc.sbuf_tensor(name, list(shape), dt))
    PSUM = lambda name, shape, dt: st.enter_context(nc.psum_tensor(name, list(shape), dt))

    em = Em(nc)

    def SEM(name):
        s = Sem(st.enter_context(nc.semaphore(name)))
        em.sems.append(s)
        return s

    s_ld = SEM("s_ld")
    s_gld = SEM("s_gld")
    s_pe = SEM("s_pe")
    s_dve = SEM("s_dve")
    s_act = SEM("s_act")
    s_st = SEM("s_st")
    s_cc = SEM("s_cc")
    s_pk = Sem(st.enter_context(nc.semaphore("s_pk")))
    em.autosem = {"vector": s_dve, "scalar": s_act}

    C = {}
    for nm, sh, dt in [("identf", [128, 128], F32), ("identb", [128, 128], BF16),
                       ("utri", [128, 128], F32), ("lastrow", [128, 128], F32),
                       ("sel16", [16, 16 * 128], F32),
                       ("ltri_sneg", [128, H * 128], F32),
                       ("utri_sneg", [128, H * 128], F32),
                       ("utri_incl", [128, H * 128], F32),
                       ("onesf_row", [1, 128], F32), ("onesf_col", [128, 1], F32),
                       ("coresel", [128, NCORES], F32),
                       ("ln1w", [128, ND], F32), ("ln1b", [128, ND], F32),
                       ("ln2w", [128, ND], F32), ("ln2b", [128, ND], F32),
                       ("b1", [128, FMLP // 128], F32), ("b2", [128, ND], F32),
                       ("identb_rep", [128, H * 128], BF16)]:
        C[nm] = SB("c_" + nm, sh, dt)
    combT = SB("combT", [16, TC], F32)
    ps_slab = SB("ps_slab", [128, H * 128], BF16)
    qs_slab = SB("qs_slab", [128, H * 128], BF16)
    s_slab = SB("s_slab", [128, H * 128], BF16)

    psA = PSUM("psA", [128, 1024], F32)
    psB = PSUM("psB", [128, 1024], F32)
    psC = PSUM("psC", [128, 512], F32)
    psD = PSUM("psD", [128, 512], F32)
    psE = PSUM("psE", [128, 512], F32)
    psF = PSUM("psF", [128, 512], F32)

    for nm in C:
        em.op("sync", lambda e, d=C[nm], s=dram[nm]: e.dma_start(out=d[:, :], in_=s),
              inc=s_ld, inck=16)
    em.flush()

    stA = ExitStack()
    SBA = lambda name, shape, dt: stA.enter_context(nc.sbuf_tensor(name, list(shape), dt))

    wq_sb = SBA("wq_sb", [128, ND * D], BF16)
    wk_sb = SBA("wk_sb", [128, ND * D], BF16)
    wv_sb = SBA("wv_sb", [128, ND * D], BF16)
    wab_sb = SBA("wab_sb", [128, ND * 16], BF16)
    rw_sb = SBA("rw_sb", [128, ND * NE], F32)

    xtm = [SBA(f"xtm{i}", [128, D], F32) for i in range(2)]
    xnf = SBA("xnf", [128, D], F32)
    basef = SBA("basef", [128, D], F32)
    ntmp = SBA("ntmp", [128, D], F32)
    n_tg = SBA("n_tg", [128, ND * TGT], BF16)
    q_tg = SBA("q_tg", [128, H * TGT], BF16)
    k_tg = SBA("k_tg", [128, H * TGT], BF16)
    qs_f = None
    qs_f = ntmp[:, 0:TGT]
    q2_f = basef[:, 0:TGT]
    rowb = SBA("rowb", [1, TGT], F32)
    colb = SBA("colb", [128, 16], F32)
    logit = SBA("logit", [128, 16], F32)
    ptmp = SBA("ptmp", [128, 16], F32)
    ptmp2 = SBA("ptmp2", [128, 16], F32)
    ltT = SBA("ltT", [16, 128], F32)
    softc = SBA("softc", [128, 4], F32)
    PR = 2
    v_tm = [SBA(f"v_tm{p}", [128, H * 128], BF16) for p in range(PR)]
    k_tm = [SBA(f"k_tm{p}", [128, H * 128], BF16) for p in range(PR)]
    gk_tm = [SBA(f"gk_tm{p}", [128, H * 128], BF16) for p in range(PR)]
    kp_tm = [SBA(f"kp_tm{p}", [128, H * 128], BF16) for p in range(PR)]
    zab = [SBA(f"zab{p}", [128, 16], F32) for p in range(PR)]
    labuf = [SBA(f"labuf{p}", [128, 24], F32) for p in range(PR)]
    lamT = [SBA(f"lamT{p}", [16, 128], F32) for p in range(PR)]
    explamT = [SBA(f"explamT{p}", [16, 128], F32) for p in range(PR)]
    neglamT = [SBA(f"neglamT{p}", [16, 128], F32) for p in range(PR)]
    gamcol = [SBA(f"gamcol{p}", [128, 8], F32) for p in range(PR)]
    gccol = [SBA(f"gccol{p}", [128, 8], F32) for p in range(PR)]
    sccol = [SBA(f"sccol{p}", [128, 8], F32) for p in range(PR)]
    mbneg = [SBA(f"mbneg{p}", [128, 8], F32) for p in range(PR)]
    rfull = [SBA(f"rfull{p}", [128, H * 128], BF16) for p in range(PR)]
    rfullT = [SBA(f"rfullT{p}", [128, H * 128], BF16) for p in range(PR)]
    rfs = SBA("rfs", [128, H * 128], F32)
    ntsl = [SBA(f"ntsl{p}", [128, H * 128], BF16) for p in range(PR)]
    atsl = [SBA(f"atsl{p}", [128, H * 128], BF16) for p in range(PR)]
    chain = [[SBA(f"chain{p}_{i}", [128, H * 128], BF16) for i in range(2)] for p in range(PR)]
    nsl = [chain[p][0] for p in range(PR)]
    w_tm = [SBA(f"w_tm{p}", [128, H * 128], BF16) for p in range(PR)]
    u_tm = [SBA(f"u_tm{p}", [128, H * 128], BF16) for p in range(PR)]
    zt_sl = [SBA(f"zt_sl{p}", [128, H * 128], BF16) for p in range(PR)]
    pt_sl = [SBA(f"pt_sl{p}", [128, H * 128], BF16) for p in range(PR)]
    qa_sl = [SBA(f"qa_sl{p}", [128, H * 128], BF16) for p in range(PR)]
    zabT_tg = SBA("zabT_tg", [16, TGT], F32)

    RDY = {}

    def mark(nm, sem):
        RDY[nm] = (sem, sem.n)

    def need(eng, *nms):
        for nm in nms:
            if nm in RDY:
                s, v = RDY[nm]
                em.wait(eng, s, v)

    s_wl = Sem(st.enter_context(nc.semaphore("s_wl")))
    wstage = SBA("wstage", [128, D], F32)
    wsth = [wstage[:, 0:512], wstage[:, 512:1024]]
    for k in range(ND):
        em.op("sync", lambda e, k=k: e.dma_start(
            out=rw_sb[:, k * NE:(k + 1) * NE],
            in_=dram["rw"][k * 128:(k + 1) * 128, :]), inc=s_wl, inck=16)
    RDY["w_rw"] = (s_wl, s_wl.n)
    cast_jobs = []
    for nm, dst, m in [("wq", wq_sb, D), ("wk", wk_sb, D), ("wab", wab_sb, 16), ("wv", wv_sb, D)]:
        for k in range(ND):
            if m == D:
                cast_jobs.append((nm, dst, k, 0, 512))
                cast_jobs.append((nm, dst, k, 512, 512))
            else:
                cast_jobs.append((nm, dst, k, 0, m))
    cast_done = {}
    last = {}
    for j, (nm, dst, k, co, w) in enumerate(cast_jobs):
        sg = wsth[j % 2]
        if j >= 2:
            em.wait("sync", s_act, cast_done[j - 2])
        em.op("sync", lambda e, sg=sg, nm=nm, k=k, co=co, w=w: e.dma_start(
            out=sg[:, 0:w], in_=dram[nm][k * 128:(k + 1) * 128, co:co + w]),
            inc=s_wl, inck=16)
        em.wait("scalar", s_wl, s_wl.n)
        mw = D if w == 512 else w
        em.op("scalar", lambda e, sg=sg, dst=dst, k=k, co=co, w=w, mw=mw: e.activation(
            dst[:, k * mw + co:k * mw + co + w], sg[:, 0:w], AF.Identity))
        cast_done[j] = s_act.n
        last[nm] = True
        if j + 1 == len(cast_jobs) or cast_jobs[j + 1][0] != nm:
            mark("w_" + nm, s_act)
    pack_jobs = []
    for ft in range(NFT):
        pack_jobs.append(("c", dram["wgu"], ft * 128, wgu_t, ft))
        pack_jobs.append(("c", dram["wgu"], NMID + ft * 128, wgu_t, NFT + ft))
    for kt in range(NFT):
        pack_jobs.append(("r", dram["wd"], kt * 128, wd_t, kt))
    for ft in range(FMLP // 128):
        pack_jobs.append(("c", dram["w1"], ft * 128, w1_t, ft))
    for kt in range(FMLP // 128):
        pack_jobs.append(("r", dram["w2"], kt * 128, w2_t, kt))
    pack_cursor = [0]

    def emit_packs(n):
        lo = pack_cursor[0]
        hi = min(lo + n, len(pack_jobs))
        pack_cursor[0] = hi
        for kind, srcap, off, dstap, idx in pack_jobs[lo:hi]:
            if kind == "c":
                em.op("gpsimd", lambda e, srcap=srcap, off=off, dstap=dstap, idx=idx: e.dma_start(
                    out=dstap[idx, :, :],
                    in_=srcap[:, off:off + 128].rearrange("(k p) m -> p k m", p=128)),
                    inc=s_pk, inck=16)
            else:
                em.op("gpsimd", lambda e, srcap=srcap, off=off, dstap=dstap, idx=idx: e.dma_start(
                    out=dstap[idx, :, :], in_=srcap[off:off + 128, :]), inc=s_pk, inck=16)

    def p1_chunk(c):
        par = c % 2
        buf = xtm[par]
        BIG = psA if par == 0 else psB
        o = par * 8
        need("sync", f"xtmfree{par}")
        em.op("sync", lambda e, b=buf, c=c: e.dma_start(
            out=b[:, :], in_=x_in[c * 128:(c + 1) * 128, :]), inc=s_ld, inck=16)
        ld_x = s_ld.n
        em.wait("vector", s_ld, ld_x)
        need("vector", f"BIGfree{par}", f"colbfree{par}")
        em.op("vector", lambda e, b=buf, o=o: e.tensor_reduce(colb[:, o + 0:o + 1], b[:, :], AX, ALU.add))
        em.op("vector", lambda e, b=buf, BIG=BIG: e.tensor_tensor(BIG[:, :], b[:, :], b[:, :], ALU.mult))
        em.op("vector", lambda e, o=o, BIG=BIG: e.tensor_reduce(colb[:, o + 1:o + 2], BIG[:, :], AX, ALU.add))
        dve_x2 = s_dve.n
        em.op("vector", lambda e, o=o: e.tensor_scalar_mul(colb[:, o + 0:o + 1], colb[:, o + 0:o + 1], 1.0 / D))
        em.op("vector", lambda e, o=o: e.tensor_scalar_mul(colb[:, o + 1:o + 2], colb[:, o + 1:o + 2], 1.0 / D))
        em.op("vector", lambda e, o=o: e.tensor_tensor(colb[:, o + 2:o + 3], colb[:, o + 0:o + 1], colb[:, o + 0:o + 1], ALU.mult))
        em.op("vector", lambda e, o=o: e.tensor_tensor(colb[:, o + 2:o + 3], colb[:, o + 1:o + 2], colb[:, o + 2:o + 3], ALU.subtract))
        em.op("vector", lambda e, o=o: e.tensor_scalar_add(colb[:, o + 3:o + 4], colb[:, o + 2:o + 3], EPS))
        em.op("vector", lambda e, o=o: e.reciprocal(colb[:, o + 4:o + 5], colb[:, o + 3:o + 4]), inc=s_dve)
        dve_stats = s_dve.n
        em.wait("scalar", s_dve, dve_stats)
        em.op("scalar", lambda e, o=o: e.activation(colb[:, o + 5:o + 6], colb[:, o + 4:o + 5], AF.Sqrt))
        em.op("scalar", lambda e, o=o: e.activation(colb[:, o + 6:o + 7], colb[:, o + 2:o + 3], AF.Identity,
                                                    scale=colb[:, o + 4:o + 5]), inc=s_act)
        act_r1 = s_act.n
        em.wait("vector", s_act, act_r1)
        em.op("vector", lambda e, o=o: e.tensor_scalar_add(colb[:, o + 6:o + 7], colb[:, o + 6:o + 7], EPS))
        em.op("vector", lambda e, o=o: e.reciprocal(colb[:, o + 6:o + 7], colb[:, o + 6:o + 7]))
        em.op("vector", lambda e, o=o: e.scalar_tensor_tensor(colb[:, o + 7:o + 8], colb[:, o + 0:o + 1], -1.0,
                                                              colb[:, o + 5:o + 6], ALU.mult, ALU.mult), inc=s_dve)
        dve_s2 = s_dve.n
        em.wait("scalar", s_dve, dve_s2)
        em.op("scalar", lambda e, o=o: e.activation(colb[:, o + 6:o + 7], colb[:, o + 6:o + 7], AF.Sqrt))
        need("scalar", "ntmp_pe")
        em.op("scalar", lambda e, b=buf, o=o: e.activation(ntmp[:, :], b[:, :], AF.Identity,
                                                           scale=colb[:, o + 5:o + 6],
                                                           bias=colb[:, o + 7:o + 8]), inc=s_act)
        act_xn = s_act.n
        need("tensor", f"BIGfree{par}")
        em.wait("tensor", s_ld, ld_x)
        em.wait("tensor", s_dve, dve_x2)
        for dt in range(ND):
            em.op("tensor", lambda e, dt=dt, b=buf, BIG=BIG: e.transpose(
                BIG[:, dt * 128:(dt + 1) * 128], b[:, dt * 128:(dt + 1) * 128], C["identf"][:, :]),
                inc=(s_pe if dt == ND - 1 else None))
        pe_xT = s_pe.n
        em.wait("vector", s_pe, pe_xT)
        need("vector", "basef_free")
        em.op("vector", lambda e, BIG=BIG: e.tensor_copy(basef[:, :], BIG[:, :]), inc=s_dve)
        dve_bc = s_dve.n
        em.wait("tensor", s_act, act_xn)
        em.wait("tensor", s_dve, dve_bc)
        for dt in range(ND):
            em.op("tensor", lambda e, dt=dt, BIG=BIG: e.transpose(
                BIG[:, dt * 128:(dt + 1) * 128], ntmp[:, dt * 128:(dt + 1) * 128], C["identf"][:, :]),
                inc=(s_pe if dt == ND - 1 else None))
        pe_xnT = s_pe.n
        RDY["ntmp_pe"] = (s_pe, pe_xnT)
        em.wait("vector", s_pe, pe_xnT)
        need("vector", "xnf_free", "xnf_free2")
        em.op("vector", lambda e, BIG=BIG: e.tensor_copy(xnf[:, :], BIG[:, :]))
        em.op("vector", lambda e: e.tensor_tensor(basef[:, :], basef[:, :], xnf[:, :], ALU.add), inc=s_dve)
        dve_xnf = s_dve.n
        em.wait("sync", s_dve, dve_xnf)
        em.op("sync", lambda e, c=c: e.dma_start(out=xn_hbm[c, :, :], in_=xnf[:, :]), inc=s_st, inck=16)
        em.op("sync", lambda e, c=c: e.dma_start(out=base_hbm[c, :, :], in_=basef[:, :]), inc=s_st, inck=16)
        RDY["basef_free"] = (s_st, s_st.n)
        RDY["xnf_free2"] = (s_st, s_st.n)
        em.wait("scalar", s_pe, pe_xT)
        em.op("scalar", lambda e, b=buf, o=o: e.activation(b[:, :], ntmp[:, :], AF.Identity,
                                                           scale=colb[:, o + 6:o + 7]), inc=s_act)
        act_n = s_act.n
        RDY[f"colbfree{par}"] = (s_act, act_n)
        need("tensor", "w_rw", "ONEfree1")
        em.wait("tensor", s_dve, dve_xnf)
        for dt in range(ND):
            em.op("tensor", lambda e, dt=dt: e.matmul(
                psD[0:16, 0:128], rw_sb[:, dt * NE:(dt + 1) * NE], xnf[:, dt * 128:(dt + 1) * 128],
                start=(dt == 0), stop=(dt == ND - 1)), inc=(s_pe if dt == ND - 1 else None))
        pe_rt = s_pe.n
        RDY["xnf_free"] = (s_pe, pe_rt)
        em.wait("tensor", s_act, act_n)
        for dt in range(ND):
            em.op("tensor", lambda e, dt=dt, b=buf, BIG=BIG: e.transpose(
                BIG[:, dt * 128:(dt + 1) * 128], b[:, dt * 128:(dt + 1) * 128], C["identf"][:, :]),
                inc=(s_pe if dt == ND - 1 else None))
        pe_nT = s_pe.n
        RDY[f"xtmfree{par}"] = (s_pe, pe_nT)
        tgo = (c % 4) * 128
        em.wait("scalar", s_pe, pe_nT)
        need("scalar", "tgfree_pe")
        for dt in range(ND):
            em.op("scalar", lambda e, dt=dt, tgo=tgo, BIG=BIG: e.activation(
                n_tg[:, dt * TGT + tgo: dt * TGT + tgo + 128], BIG[:, dt * 128:(dt + 1) * 128],
                AF.Identity, scale=C["ln1w"][:, dt:dt + 1], bias=C["ln1b"][:, dt:dt + 1]),
                inc=(s_act if dt == ND - 1 else None))
        mark(f"n{c % 4}", s_act)
        mark(f"BIGfree{par}", s_act)
        em.wait("vector", s_pe, pe_rt)
        need("vector", "router_free")
        em.op("vector", lambda e: e.tensor_copy(ltT[:, :], psD[0:16, 0:128]), inc=s_dve)
        dve_lt = s_dve.n
        em.wait("tensor", s_dve, dve_lt)
        em.op("tensor", lambda e: e.transpose(psD[:, 128:144], ltT[:, :], C["identf"][0:16, 0:16]), inc=s_pe)
        pe_lt = s_pe.n
        em.wait("vector", s_pe, pe_lt)
        em.op("vector", lambda e: e.tensor_copy(logit[:, :], psD[:, 128:144]))
        em.op("vector", lambda e: e.tensor_reduce(softc[:, 0:1], logit[:, :], AX, ALU.max))
        em.op("vector", lambda e: e.tensor_scalar_mul(softc[:, 0:1], softc[:, 0:1], -1.0), inc=s_dve)
        em.wait("scalar", s_dve)
        em.op("scalar", lambda e: e.activation(ptmp[:, :], logit[:, :], AF.Exp, bias=softc[:, 0:1]), inc=s_act)
        em.wait("vector", s_act)
        em.op("vector", lambda e: e.tensor_reduce(softc[:, 1:2], ptmp[:, :], AX, ALU.add))
        em.op("vector", lambda e: e.reciprocal(softc[:, 1:2], softc[:, 1:2]))
        em.op("vector", lambda e: e.tensor_scalar(ptmp[:, :], ptmp[:, :], softc[:, 1:2], None, ALU.mult))
        em.op("vector", lambda e: e.tensor_reduce(softc[:, 0:1], ptmp[:, :], AX, ALU.max))
        em.op("vector", lambda e: e.tensor_scalar(ptmp2[:, :], ptmp[:, :], softc[:, 0:1], None, ALU.is_ge))
        em.op("vector", lambda e: e.scalar_tensor_tensor(ptmp2[:, :], ptmp2[:, :], -1e9, ptmp[:, :], ALU.mult, ALU.add))
        em.op("vector", lambda e: e.tensor_reduce(softc[:, 2:3], ptmp2[:, :], AX, ALU.max))
        em.op("vector", lambda e: e.tensor_scalar(ptmp2[:, :], ptmp[:, :], softc[:, 2:3], None, ALU.is_ge))
        em.op("vector", lambda e: e.tensor_tensor(ptmp2[:, :], ptmp2[:, :], ptmp[:, :], ALU.mult))
        em.op("vector", lambda e: e.tensor_tensor(softc[:, 3:4], softc[:, 0:1], softc[:, 2:3], ALU.add))
        em.op("vector", lambda e: e.tensor_scalar_add(softc[:, 3:4], softc[:, 3:4], 1e-9))
        em.op("vector", lambda e: e.reciprocal(softc[:, 3:4], softc[:, 3:4]))
        em.op("vector", lambda e: e.tensor_scalar(ptmp2[:, :], ptmp2[:, :], softc[:, 3:4], None, ALU.mult), inc=s_dve)
        dve_p2 = s_dve.n
        em.wait("tensor", s_dve, dve_p2)
        em.op("tensor", lambda e: e.transpose(psD[0:16, 256:384], ptmp2[:, :], C["identf"][:, :]), inc=s_pe)
        pe_p2 = s_pe.n
        RDY["router_free"] = (s_pe, pe_p2)
        em.wait("vector", s_pe, pe_p2)
        em.op("vector", lambda e, c=c: e.tensor_copy(combT[:, c * 128:(c + 1) * 128], psD[0:16, 256:384]), inc=s_dve)
        mark("ONEfree1", s_dve)
    qs2 = [ntmp[:, 0:512], ntmp[:, 512:1024]]
    q22 = [basef[:, 0:512], basef[:, 512:1024]]

    def p2_tg_proj(tg):
        need("tensor", "w_wq", "w_wk", "w_wab", "n0", "n1", "n2", "n3")
        need("tensor", "ONEfree0", "ONEfree1", "AUXfree0", "AUXfree1")
        need("vector", "tgfree_pe", "tgfree_dve", "tgfree_act")
        need("scalar", "tgfree_pe", "tgfree_dve", "tgfree_act")
        need("scalar", "ntmp_pe")
        need("vector", "basef_free")
        pb_free = {}
        qs_free = {}
        sum_free = 0
        rep_free = 0
        rowb_free = 0
        idx = 0
        for wsb, dst in ((wq_sb, q_tg), (wk_sb, k_tg)):
            for h in range(H):
                bp = idx % 2
                ps = psC if bp == 0 else psD
                if bp in pb_free:
                    em.wait("tensor", s_act, pb_free[bp])
                for dt in range(ND):
                    em.op("tensor", lambda e, dt=dt, h=h, wsb=wsb, ps=ps: e.matmul(
                        ps[:, :], wsb[:, dt * D + h * 128: dt * D + (h + 1) * 128],
                        n_tg[:, dt * TGT:(dt + 1) * TGT],
                        start=(dt == 0), stop=(dt == ND - 1)),
                        inc=(s_pe if dt == ND - 1 else None))
                pe_mm = s_pe.n
                em.wait("scalar", s_pe, pe_mm)
                if bp in qs_free:
                    em.wait("scalar", s_dve, qs_free[bp])
                em.op("scalar", lambda e, ps=ps, bp=bp: e.activation(qs2[bp][:, :], ps[:, :], AF.Silu),
                      inc=s_act)
                pb_free[bp] = s_act.n
                em.wait("vector", s_act, pb_free[bp])
                em.op("vector", lambda e, bp=bp: e.tensor_tensor(q22[bp][:, :], qs2[bp][:, :], qs2[bp][:, :], ALU.mult),
                      inc=s_dve)
                dve_sq = s_dve.n
                em.wait("tensor", s_dve, dve_sq)
                if sum_free:
                    em.wait("tensor", s_dve, sum_free)
                em.op("tensor", lambda e, bp=bp: e.matmul(psE[0:1, 0:TGT], C["onesf_col"][:, :], q22[bp][:, :],
                                                          start=True, stop=True), inc=s_pe)
                pe_sum = s_pe.n
                em.wait("vector", s_pe, pe_sum)
                if rowb_free:
                    em.wait("vector", s_pe, rowb_free)
                em.op("vector", lambda e: e.tensor_scalar_add(rowb[0:1, :], psE[0:1, 0:TGT], 1e-6))
                sum_free = s_dve.n
                em.op("vector", lambda e: e.reciprocal(rowb[0:1, :], rowb[0:1, :]), inc=s_dve)
                em.wait("scalar", s_dve)
                em.op("scalar", lambda e: e.activation(rowb[0:1, :], rowb[0:1, :], AF.Sqrt), inc=s_act)
                act_sqrt = s_act.n
                em.wait("tensor", s_act, act_sqrt)
                if rep_free:
                    em.wait("tensor", s_dve, rep_free)
                em.op("tensor", lambda e: e.matmul(psF[:, 0:TGT], C["onesf_row"][:, :], rowb[0:1, :],
                                                   start=True, stop=True), inc=s_pe)
                pe_rep = s_pe.n
                rowb_free = pe_rep
                em.wait("vector", s_pe, pe_rep)
                em.op("vector", lambda e, h=h, dst=dst, bp=bp: e.tensor_tensor(
                    dst[:, h * TGT:(h + 1) * TGT], qs2[bp][:, :], psF[:, 0:TGT], ALU.mult), inc=s_dve)
                qs_free[bp] = s_dve.n
                rep_free = s_dve.n
                idx += 1
        mark("ONEfree0", s_act)
        mark("ONEfree1", s_act)
        mark("AUXfree0", s_dve)
        RDY["ntmp_pe"] = (s_dve, qs_free[1])
        RDY["basef_free"] = (s_pe, pe_sum)
        em.wait("tensor", s_dve, rep_free)
        for dt in range(ND):
            em.op("tensor", lambda e, dt=dt: e.matmul(
                psF[0:16, 0:TGT], wab_sb[:, dt * 16:(dt + 1) * 16], n_tg[:, dt * TGT:(dt + 1) * TGT],
                start=(dt == 0), stop=(dt == ND - 1)), inc=(s_pe if dt == ND - 1 else None))
        pe_zab = s_pe.n
        em.wait("vector", s_pe, pe_zab)
        em.op("vector", lambda e: e.tensor_copy(zabT_tg[:, :], psF[0:16, 0:TGT]), inc=s_dve)
        mark("AUXfree1", s_dve)
        mark("qk", s_dve)

    def hsl(t, h):
        return t[:, h * 128:(h + 1) * 128]

    def st_v(c):
        pr = c % 2
        BIG = psA if pr == 0 else psB
        ONE = psC if pr == 0 else psD
        ci = c % 4
        nch = lambda dt: n_tg[:, dt * TGT + ci * 128: dt * TGT + (ci + 1) * 128]
        kch = lambda h: k_tg[:, h * TGT + ci * 128: h * TGT + (ci + 1) * 128]
        need("tensor", f"BIGfree{pr}", f"ONEfree{pr}", "qk", "w_wv", f"n{ci}")
        for half in range(2):
            for dt in range(ND):
                em.op("tensor", lambda e, dt=dt, half=half, BIG=BIG, nch=nch: e.matmul(
                    BIG[:, half * 512:(half + 1) * 512], nch(dt),
                    wv_sb[:, dt * D + half * 512: dt * D + (half + 1) * 512],
                    start=(dt == 0), stop=(dt == ND - 1)),
                    inc=(s_pe if dt == ND - 1 else None))
        pe_v = s_pe.n
        em.wait("scalar", s_pe, pe_v)
        need("scalar", f"vwar{pr}")
        em.op("scalar", lambda e, BIG=BIG, pr=pr: e.activation(v_tm[pr][:, 0:512], BIG[:, 0:512], AF.Silu))
        em.op("scalar", lambda e, BIG=BIG, pr=pr: e.activation(v_tm[pr][:, 512:1024], BIG[:, 512:1024], AF.Silu))
        mark(f"v{pr}", s_act)
        mark(f"BIGfree{pr}", s_act)
        ONEb = ONE[:, :].bitcast(BF16)
        need("tensor", f"ONEfree{pr}")
        for h in range(H):
            em.op("tensor", lambda e, h=h, ONEb=ONEb, kch=kch: e.transpose(
                hsl(ONEb, h), kch(h), C["identb"][:, :]), inc=(s_pe if h == H - 1 else None))
        em.wait("vector", s_pe)
        need("vector", f"ktmwar{pr}")
        em.op("vector", lambda e, ONEb=ONEb, pr=pr: e.tensor_copy(k_tm[pr][:, :], ONEb[:, :]))
        mark(f"ktm{pr}", s_dve)
        mark(f"ONEfree{pr}", s_dve)

    def st_la(c):
        pr = c % 2
        AUX = psE if pr == 0 else psF
        ci = c % 4
        la_, zab_, lamT_, explamT_, neglamT_ = labuf[pr], zab[pr], lamT[pr], explamT[pr], neglamT[pr]
        need("tensor", f"AUXfree{pr}", "qk")
        em.op("tensor", lambda e, AUX=AUX, ci=ci: e.transpose(
            AUX[:, 0:16], zabT_tg[0:16, ci * 128:(ci + 1) * 128],
            C["identf"][0:16, 0:16]), inc=s_pe)
        pe1 = s_pe.n
        em.wait("vector", s_pe, pe1)
        need("vector", f"lawar{pr}", f"rscal{pr}")
        em.op("vector", lambda e, AUX=AUX, zab_=zab_: e.tensor_copy(zab_[:, :], AUX[:, 0:16]))
        em.wait("scalar", s_dve)
        em.op("scalar", lambda e, la_=la_, zab_=zab_: e.activation(la_[:, 0:16], zab_[:, :], AF.Abs))
        em.op("scalar", lambda e, la_=la_: e.activation(la_[:, 0:16], la_[:, 0:16], AF.Exp, scale=-1.0))
        em.op("scalar", lambda e, la_=la_: e.activation(la_[:, 0:16], la_[:, 0:16], AF.Ln, bias=1.0))
        em.wait("vector", s_act)
        em.op("vector", lambda e, zab_=zab_: e.tensor_scalar(zab_[:, :], zab_[:, :], 0.0, None, ALU.min))
        em.op("vector", lambda e, la_=la_, zab_=zab_: e.tensor_tensor(la_[:, 0:16], zab_[:, :], la_[:, 0:16], ALU.subtract))
        dve1 = s_dve.n
        em.wait("tensor", s_dve, dve1)
        em.op("tensor", lambda e, AUX=AUX, la_=la_: e.matmul(AUX[:, 16:24], C["utri"][:, :], la_[:, 0:8],
                                                             start=True, stop=True), inc=s_pe)
        pe2 = s_pe.n
        em.wait("vector", s_pe, pe2)
        em.op("vector", lambda e, AUX=AUX, la_=la_: e.tensor_copy(la_[:, 16:24], AUX[:, 16:24]))
        em.op("vector", lambda e, la_=la_: e.tensor_tensor(la_[:, 8:16], la_[:, 16:24], la_[:, 8:16], ALU.subtract))
        em.op("vector", lambda e, la_=la_, pr=pr: e.tensor_scalar_mul(mbneg[pr][:, :], la_[:, 8:16], -1.0))
        dve2 = s_dve.n
        em.wait("tensor", s_dve, dve2)
        em.op("tensor", lambda e, AUX=AUX, la_=la_: e.matmul(AUX[:, 24:32], C["lastrow"][:, :], la_[:, 16:24],
                                                             start=True, stop=True))
        em.op("tensor", lambda e, AUX=AUX, la_=la_: e.transpose(AUX[0:16, 32:160], la_[:, 8:24], C["identf"][:, :]),
              inc=s_pe)
        pe3 = s_pe.n
        em.wait("vector", s_pe, pe3)
        em.op("vector", lambda e, AUX=AUX, lamT_=lamT_: e.tensor_copy(lamT_[:, :], AUX[0:16, 32:160]))
        em.op("vector", lambda e, AUX=AUX, pr=pr: e.tensor_copy(gccol[pr][:, :], AUX[:, 24:32]))
        em.op("vector", lambda e, lamT_=lamT_, neglamT_=neglamT_: e.tensor_scalar_mul(neglamT_[:, :], lamT_[:, :], -1.0))
        em.op("vector", lambda e, la_=la_, pr=pr: e.tensor_tensor(sccol[pr][:, :], gccol[pr][:, :], la_[:, 8:16], ALU.subtract))
        mark(f"AUXfree{pr}", s_dve)
        em.wait("scalar", s_dve)
        em.op("scalar", lambda e, lamT_=lamT_, explamT_=explamT_: e.activation(explamT_[:, :], lamT_[:, :], AF.Exp))
        em.op("scalar", lambda e, la_=la_, pr=pr: e.activation(gamcol[pr][:, :], la_[:, 16:24], AF.Exp))
        em.op("scalar", lambda e, pr=pr: e.activation(sccol[pr][:, :], sccol[pr][:, :], AF.Exp))
        em.op("scalar", lambda e, pr=pr: e.activation(gccol[pr][:, :], gccol[pr][:, :], AF.Exp))
        sc1 = s_act.n
        em.wait("vector", s_act, sc1)
        need("vector", f"ktm{pr}")
        for h in range(H):
            em.op("vector", lambda e, h=h, pr=pr: e.tensor_scalar(
                hsl(gk_tm[pr], h), hsl(k_tm[pr], h), gamcol[pr][:, h:h + 1], None, ALU.mult))
        for h in range(H):
            em.op("vector", lambda e, h=h, pr=pr: e.tensor_scalar(
                hsl(kp_tm[pr], h), hsl(k_tm[pr], h), sccol[pr][:, h:h + 1], None, ALU.mult))
        mark(f"la{pr}", s_dve)
        mark(f"gkkp{pr}", s_dve)

    def st_R(c):
        pr = c % 2
        BIG = psA if pr == 0 else psB
        need("tensor", f"BIGfree{pr}", f"la{pr}")
        for h in range(H):
            em.op("tensor", lambda e, h=h, BIG=BIG, pr=pr: e.matmul(
                hsl(BIG, h), C["sel16"][:, h * 128:(h + 1) * 128],
                neglamT[pr][:, :], start=True, stop=True), inc=(s_pe if h == H - 1 else None))
        pe1 = s_pe.n
        em.wait("vector", s_pe, pe1)
        need("vector", "rscal0", "rscal1")
        em.op("vector", lambda e, BIG=BIG, pr=pr: e.tensor_tensor(rfs[:, :], BIG[:, :], C["ltri_sneg"][:, :], ALU.add))
        dve1 = s_dve.n
        em.wait("scalar", s_dve, dve1)
        for h in range(H):
            em.op("scalar", lambda e, h=h, pr=pr: e.activation(
                hsl(rfull[pr], h), hsl(rfs, h), AF.Exp, bias=labuf[pr][:, 16 + h:17 + h]))
        mark(f"rf{pr}", s_act)
        em.wait("tensor", s_dve, dve1)
        for h in range(H):
            em.op("tensor", lambda e, h=h, BIG=BIG, pr=pr: e.matmul(
                hsl(BIG, h), C["sel16"][:, (8 + h) * 128:(9 + h) * 128],
                lamT[pr][:, :], start=True, stop=True), inc=(s_pe if h == H - 1 else None))
        pe2 = s_pe.n
        em.wait("vector", s_pe, pe2)
        em.wait("vector", s_act)
        em.op("vector", lambda e, BIG=BIG, pr=pr: e.tensor_tensor(rfs[:, :], BIG[:, :], C["utri_incl"][:, :], ALU.add))
        dve2 = s_dve.n
        mark(f"BIGfree{pr}", s_dve)
        em.wait("scalar", s_dve, dve2)
        for h in range(H):
            em.op("scalar", lambda e, h=h, pr=pr: e.activation(
                hsl(rfullT[pr], h), hsl(rfs, h), AF.Exp, bias=mbneg[pr][:, h:h + 1]))
        mark(f"rfT{pr}", s_act)
        mark(f"rscal{pr}", s_act)

    def st_NAT(c):
        pr = c % 2
        BIG = psA if pr == 0 else psB
        ci = c % 4
        kch = lambda h: k_tg[:, h * TGT + ci * 128: h * TGT + (ci + 1) * 128]
        qch = lambda h: q_tg[:, h * TGT + ci * 128: h * TGT + (ci + 1) * 128]
        need("tensor", f"BIGfree{pr}", "qk")
        for h in range(H):
            em.op("tensor", lambda e, h=h, BIG=BIG, kch=kch: e.matmul(
                hsl(BIG, h), kch(h), kch(h), start=True, stop=True),
                inc=(s_pe if h == H - 1 else None))
        pe1 = s_pe.n
        em.wait("vector", s_pe, pe1)
        need("vector", f"rf{pr}", f"rfT{pr}")
        em.op("vector", lambda e, BIG=BIG, pr=pr: e.tensor_scalar_mul(w_tm[pr][:, :], BIG[:, :], -1.0))
        em.op("vector", lambda e, pr=pr: e.tensor_tensor(nsl[pr][:, :], rfull[pr][:, :], w_tm[pr][:, :], ALU.mult))
        em.op("vector", lambda e, pr=pr: e.tensor_tensor(u_tm[pr][:, :], rfullT[pr][:, :], C["utri_sneg"][:, :], ALU.mult))
        em.op("vector", lambda e, BIG=BIG, pr=pr: e.tensor_tensor(ntsl[pr][:, :], u_tm[pr][:, :], BIG[:, :], ALU.mult))
        dve1 = s_dve.n
        em.wait("tensor", s_dve, dve1)
        for h in range(H):
            em.op("tensor", lambda e, h=h, BIG=BIG, kch=kch, qch=qch: e.matmul(
                hsl(BIG, h), kch(h), qch(h), start=True, stop=True),
                inc=(s_pe if h == H - 1 else None))
        pe2 = s_pe.n
        em.wait("vector", s_pe, pe2)
        em.op("vector", lambda e, BIG=BIG, pr=pr: e.tensor_tensor(atsl[pr][:, :], rfullT[pr][:, :], BIG[:, :], ALU.mult))
        mark(f"AT{pr}", s_dve)
        mark(f"BIGfree{pr}", s_dve)
        em.op("vector", lambda e, pr=pr: e.tensor_tensor(chain[pr][1][:, :], ntsl[pr][:, :], C["identb_rep"][:, :], ALU.add))
        mark(f"NAT{pr}", s_dve)

    chain_state = {}

    def st_chain_k(c, k):
        pr = c % 2
        BIG = psA if pr == 0 else psB
        if k == 1:
            chain_state[pr] = dict(pcur=nsl[pr], mcur=ntsl[pr], tcur=chain[pr][1])
        cs = chain_state[pr]
        pcur, mcur, tcur = cs["pcur"], cs["mcur"], cs["tcur"]
        AUX = psE if pr == 0 else psF
        need("tensor", f"BIGfree{pr}", f"NAT{pr}", f"chev{pr}", f"AUXfree{pr}")
        em.wait("tensor", s_act)
        em.wait("tensor", s_dve)
        for h in range(H):
            em.op("tensor", lambda e, h=h, BIG=BIG, mcur=mcur, pcur=pcur: e.matmul(
                hsl(BIG, h), hsl(mcur, h), hsl(pcur, h), start=True, stop=True),
                inc=(s_pe if h == H - 1 else None))
        pe_p = s_pe.n
        if k < 6:
            for h in range(4):
                em.op("tensor", lambda e, h=h, AUX=AUX, mcur=mcur, pcur=pcur: e.matmul(
                    AUX[:, h * 128:(h + 1) * 128], hsl(pcur, h), hsl(mcur, h),
                    start=True, stop=True), inc=(s_pe if h == 3 else None))
            pe_ma = s_pe.n
            em.wait("scalar", s_pe, pe_ma)
            em.op("scalar", lambda e, AUX=AUX, mcur=mcur: e.activation(mcur[:, 0:512], AUX[:, :], AF.Identity))
            act_ma = s_act.n
            em.wait("tensor", s_act, act_ma)
            for h in range(4, H):
                em.op("tensor", lambda e, h=h, AUX=AUX, mcur=mcur, pcur=pcur: e.matmul(
                    AUX[:, (h - 4) * 128:(h - 3) * 128], hsl(pcur, h), hsl(mcur, h),
                    start=True, stop=True), inc=(s_pe if h == H - 1 else None))
            pe_mb = s_pe.n
            em.wait("scalar", s_pe, pe_mb)
            em.op("scalar", lambda e, AUX=AUX, mcur=mcur: e.activation(mcur[:, 512:1024], AUX[:, :], AF.Identity))
            mark(f"AUXfree{pr}", s_act)
            last_pread = pe_mb
        else:
            last_pread = pe_p
        em.wait("vector", s_pe, last_pread)
        need("vector", f"obwar{pr}")
        em.op("vector", lambda e, BIG=BIG, pcur=pcur: e.tensor_copy(pcur[:, :], BIG[:, :]))
        dve_p = s_dve.n
        mark(f"BIGfree{pr}", s_dve)
        need("tensor", f"BIGfree{pr}")
        em.wait("tensor", s_dve, dve_p)
        for h in range(H):
            em.op("tensor", lambda e, h=h, BIG=BIG, pcur=pcur, tcur=tcur: e.matmul(
                hsl(BIG, h), hsl(pcur, h), hsl(tcur, h), start=True, stop=True),
                inc=(s_pe if h == H - 1 else None))
        pe_a = s_pe.n
        em.wait("vector", s_pe, pe_a)
        em.op("vector", lambda e, BIG=BIG, tcur=tcur: e.tensor_tensor(
            tcur[:, :], tcur[:, :], BIG[:, :], ALU.add))
        mark(f"chev{pr}", s_dve)
        mark(f"BIGfree{pr}", s_dve)

    def st_WU(c):
        pr = c % 2
        BIG = psA if pr == 0 else psB
        tcur = chain_state[pr]["tcur"]
        need("tensor", f"BIGfree{pr}", f"ONEfree{pr}", f"gkkp{pr}", f"v{pr}")
        em.wait("tensor", s_dve)
        em.wait("tensor", s_act)
        for h in range(H):
            em.op("tensor", lambda e, h=h, BIG=BIG, tcur=tcur, pr=pr: e.matmul(
                hsl(BIG, h), hsl(tcur, h), hsl(gk_tm[pr], h), start=True, stop=True),
                inc=(s_pe if h == H - 1 else None))
        pe_w = s_pe.n
        em.wait("vector", s_pe, pe_w)
        need("vector", f"wuwar{pr}")
        em.op("vector", lambda e, BIG=BIG, pr=pr: e.tensor_copy(w_tm[pr][:, :], BIG[:, :]))
        dve_w = s_dve.n
        em.wait("tensor", s_dve, dve_w)
        for h in range(H):
            em.op("tensor", lambda e, h=h, BIG=BIG, tcur=tcur, pr=pr: e.matmul(
                hsl(BIG, h), hsl(tcur, h), hsl(v_tm[pr], h), start=True, stop=True),
                inc=(s_pe if h == H - 1 else None))
        pe_u = s_pe.n
        em.wait("scalar", s_pe, pe_u)
        em.op("scalar", lambda e, BIG=BIG, pr=pr: e.activation(u_tm[pr][:, :], BIG[:, :], AF.Identity))
        mark(f"wu{pr}", s_act)
        mark(f"BIGfree{pr}", s_act)

    def st_obase(c):
        pr = c % 2
        BIG = psA if pr == 0 else psB
        need("tensor", f"BIGfree{pr}", f"wu{pr}", f"AT{pr}")
        for h in range(H):
            em.op("tensor", lambda e, h=h, BIG=BIG, pr=pr: e.matmul(
                hsl(BIG, h), hsl(u_tm[pr], h), hsl(atsl[pr], h), start=True, stop=True),
                inc=(s_pe if h == H - 1 else None))
        pe1 = s_pe.n
        em.wait("vector", s_pe, pe1)
        need("vector", f"obwar{pr}")
        em.op("vector", lambda e, BIG=BIG, pr=pr: e.tensor_copy(chain[pr][0][:, :], BIG[:, :]))
        mark(f"BIGfree{pr}", s_dve)
        mark(f"ob{pr}", s_dve)
        need("sync", f"ob{pr}")
        em.op("sync", lambda e, c=c, pr=pr: e.dma_start(out=obase_hbm[c, :, :], in_=chain[pr][0][:, :]), inc=s_st, inck=16)
        mark(f"obwar{pr}", s_st)

    def st_zt(c):
        pr = c % 2
        BIG = psA if pr == 0 else psB
        ci = c % 4
        qch = lambda h: q_tg[:, h * TGT + ci * 128: h * TGT + (ci + 1) * 128]
        need("tensor", f"BIGfree{pr}", f"la{pr}")
        for h in range(H):
            em.op("tensor", lambda e, h=h, BIG=BIG, pr=pr: e.matmul(
                hsl(BIG, h), C["sel16"][:, (8 + h) * 128:(9 + h) * 128],
                explamT[pr][:, :], start=True, stop=True), inc=(s_pe if h == H - 1 else None))
        pe1 = s_pe.n
        em.wait("vector", s_pe, pe1)
        need("vector", f"ztwar{pr}")
        for h in range(H):
            em.op("vector", lambda e, h=h, BIG=BIG, pr=pr, qch=qch: e.tensor_tensor(
                hsl(chain[pr][1], h), qch(h), hsl(BIG, h), ALU.mult))
        dve1 = s_dve.n
        em.wait("tensor", s_dve, dve1)
        for h in range(H):
            em.op("tensor", lambda e, h=h, BIG=BIG, pr=pr: e.matmul(
                hsl(BIG, h), hsl(w_tm[pr], h), hsl(atsl[pr], h), start=True, stop=True),
                inc=(s_pe if h == H - 1 else None))
        pe2 = s_pe.n
        em.wait("vector", s_pe, pe2)
        em.op("vector", lambda e, BIG=BIG, pr=pr: e.tensor_tensor(
            zt_sl[pr][:, :], chain[pr][1][:, :], BIG[:, :], ALU.subtract))
        mark(f"BIGfree{pr}", s_dve)
        mark(f"zt{pr}", s_dve)
        need("sync", f"zt{pr}")
        em.op("sync", lambda e, c=c, pr=pr: e.dma_start(out=zt_hbm[c, :, :], in_=zt_sl[pr][:, :]), inc=s_st, inck=16)
        mark(f"ztwar{pr}", s_st)

    def st_ptq(c):
        pr = c % 2
        BIG = psA if pr == 0 else psB
        need("tensor", f"BIGfree{pr}", f"gkkp{pr}", f"wu{pr}")
        for h in range(H):
            em.op("tensor", lambda e, h=h, BIG=BIG, pr=pr: e.matmul(
                hsl(BIG, h), hsl(w_tm[pr], h), hsl(kp_tm[pr], h), start=True, stop=True),
                inc=(s_pe if h == H - 1 else None))
        pe1 = s_pe.n
        em.wait("vector", s_pe, pe1)
        need("vector", f"ptwar{pr}")
        for h in range(H):
            em.op("vector", lambda e, h=h, pr=pr: e.tensor_scalar(
                hsl(rfull[pr], h), C["identb_rep"][:, h * 128:(h + 1) * 128],
                gccol[pr][:, h:h + 1], None, ALU.mult))
        em.op("vector", lambda e, BIG=BIG, pr=pr: e.tensor_tensor(
            pt_sl[pr][:, :], rfull[pr][:, :], BIG[:, :], ALU.subtract))
        mark(f"pt{pr}", s_dve)
        mark(f"BIGfree{pr}", s_dve)
        need("sync", f"pt{pr}")
        em.op("sync", lambda e, c=c, pr=pr: e.dma_start(out=pt_hbm[c, :, :], in_=pt_sl[pr][:, :]), inc=s_st, inck=16)
        mark(f"ptwar{pr}", s_st)
        need("tensor", f"BIGfree{pr}")
        for h in range(H):
            em.op("tensor", lambda e, h=h, BIG=BIG, pr=pr: e.matmul(
                hsl(BIG, h), hsl(kp_tm[pr], h), hsl(u_tm[pr], h), start=True, stop=True),
                inc=(s_pe if h == H - 1 else None))
        pe2 = s_pe.n
        em.wait("scalar", s_pe, pe2)
        need("scalar", f"qawar{pr}")
        em.op("scalar", lambda e, BIG=BIG, pr=pr: e.activation(qa_sl[pr][:, :], BIG[:, :], AF.Identity))
        mark(f"qa{pr}", s_act)
        mark(f"BIGfree{pr}", s_act)
        need("sync", f"qa{pr}")
        em.op("sync", lambda e, c=c, pr=pr: e.dma_start(out=qa_hbm[c, :, :], in_=qa_sl[pr][:, :]), inc=s_st, inck=16)
        mark(f"qawar{pr}", s_st)

    def st_compose(c):
        pr = c % 2
        BIG = psA if pr == 0 else psB
        need("tensor", f"BIGfree{pr}", f"pt{pr}", f"qa{pr}", "psqs")
        if c == 0:
            for h in range(H):
                em.op("tensor", lambda e, h=h, BIG=BIG, pr=pr: e.matmul(
                    hsl(BIG, h), hsl(pt_sl[pr], h), C["identb"][:, :], start=True, stop=True),
                    inc=(s_pe if h == H - 1 else None))
            pe1 = s_pe.n
            em.wait("vector", s_pe, pe1)
            em.op("vector", lambda e, BIG=BIG: e.tensor_copy(ps_slab[:, :], BIG[:, :]))
            em.op("vector", lambda e, pr=pr: e.tensor_copy(qs_slab[:, :], qa_sl[pr][:, :]))
            mark("psqs", s_dve)
            mark(f"BIGfree{pr}", s_dve)
        else:
            for h in range(H):
                em.op("tensor", lambda e, h=h, BIG=BIG, pr=pr: e.matmul(
                    hsl(BIG, h), hsl(pt_sl[pr], h), hsl(ps_slab, h), start=True, stop=True),
                    inc=(s_pe if h == H - 1 else None))
            pe1 = s_pe.n
            em.wait("vector", s_pe, pe1)
            em.op("vector", lambda e, BIG=BIG: e.tensor_copy(ps_slab[:, :], BIG[:, :]))
            dve1 = s_dve.n
            em.wait("tensor", s_dve, dve1)
            for h in range(H):
                em.op("tensor", lambda e, h=h, BIG=BIG, pr=pr: e.matmul(
                    hsl(BIG, h), hsl(pt_sl[pr], h), hsl(qs_slab, h), start=True, stop=True),
                    inc=(s_pe if h == H - 1 else None))
            pe2 = s_pe.n
            em.wait("vector", s_pe, pe2)
            em.op("vector", lambda e, BIG=BIG, pr=pr: e.tensor_tensor(
                qs_slab[:, :], qa_sl[pr][:, :], BIG[:, :], ALU.add))
            mark("psqs", s_dve)
            mark(f"BIGfree{pr}", s_dve)

    P2_STAGES = ([st_v, st_la, st_R, st_NAT]
                 + [(lambda cc, kk=k: st_chain_k(cc, kk)) for k in range(1, 7)]
                 + [st_WU, st_obase, st_zt, st_ptq, st_compose])

    for tg in range(NTG):
        for c in range(tg * 4, tg * 4 + 4):
            p1_chunk(c)
        if tg == 0:
            emit_packs(46)
        p2_tg_proj(tg)
        for pair in range(2):
            c0 = tg * 4 + pair * 2
            for stg in P2_STAGES:
                stg(c0)
                stg(c0 + 1)
        emit_packs(46)
        mark("tgfree_pe", s_pe)
        mark("tgfree_dve", s_dve)
        mark("tgfree_act", s_act)
    emit_packs(len(pack_jobs))
    em.flush()
    em.join(("tensor",))
    psbf2 = psA[:, 0:512].bitcast(BF16)
    for h in range(H):
        em.op("tensor", lambda e, h=h: e.transpose(
            psbf2[:, h * 128:(h + 1) * 128], ps_slab[:, h * 128:(h + 1) * 128],
            C["identb"][:, :]), inc=(s_pe if h == H - 1 else None))
    em.wait("vector", s_pe)
    em.op("vector", lambda e: e.tensor_copy(ps_slab[:, :], psbf2[:, :]))
    em.join(("sync",))
    em.op("sync", lambda e: e.dma_start(out=ag_in[:, 0:1024], in_=ps_slab[:, :]), inc=s_st, inck=16)
    em.op("sync", lambda e: e.dma_start(out=ag_in[:, 1024:2048], in_=qs_slab[:, :]), inc=s_st, inck=16)
    em.wait("gpsimd", s_st)
    em.op("gpsimd", lambda e: e.collective_compute(
        "AllGather", ALU.bypass, replica_groups=[list(range(NCORES))],
        ins=[ag_in.opt()], outs=[ag_out.opt()]), inc=s_cc, inck=1)
    em.flush()
    stA.close()

    stB = ExitStack()
    SBB = lambda name, shape, dt: stB.enter_context(nc.sbuf_tensor(name, list(shape), dt))

    stM = ExitStack()
    SBM = lambda name, shape, dt: stM.enter_context(nc.sbuf_tensor(name, list(shape), dt))
    xn_tgb = SBM("xn_tgb", [128, ND * TGT], BF16)
    acc_tg = SBM("acc_tg", [128, ND * TGT], F32)
    h_all = SBM("h_all", [128, NFT * TGT], BF16)
    htmp = SBM("htmp", [128, TGT], BF16)
    wg_sl = [SBM(f"wg_sl{i}", [128, ND * 128], BF16) for i in range(3)]
    wu_sl = [SBM(f"wu_sl{i}", [128, ND * 128], BF16) for i in range(3)]
    wd_sl = [SBM(f"wd_sl{i}", [128, ND * 128], BF16) for i in range(2)]

    psm2 = [psA[:, 0:512], psA[:, 512:1024], psB[:, 0:512], psB[:, 512:1024],
            psC[:, :], psD[:, :], psE[:, :], psF[:, :]]

    slab_last_use = {}

    def p3_tg(tg):
        need("gpsimd", "xntgb_free")
        for ci in range(4):
            c = tg * 4 + ci
            em.op("gpsimd", lambda e, c=c, ci=ci: e.dma_start(
                out=xn_tgb[:, :].rearrange("p (d f t) -> p d f t", d=ND, f=4)[:, :, ci, :],
                in_=xn_hbm[c].rearrange("p (d t) -> p d t", d=ND)), inc=s_gld, inck=16)
        gu_ready = {}
        NBUF = 3
        def issue_gu(ft):
            bi = ft % NBUF
            lu = slab_last_use.get(("gu", bi))
            if lu is not None:
                em.wait("sync", s_pe, lu)
            em.wait("sync", s_pk, (2 * ft + 2) * 16)
            em.op("sync", lambda e, ft=ft, b=wg_sl[bi]: e.dma_start(out=b[:, :], in_=wgu_t[ft, :, :]), inc=s_ld, inck=16)
            em.op("sync", lambda e, ft=ft, b=wu_sl[bi]: e.dma_start(out=b[:, :], in_=wgu_t[NFT + ft, :, :]), inc=s_ld, inck=16)
            gu_ready[ft] = s_ld.n
        for ft in range(min(NBUF, NFT)):
            issue_gu(ft)
        em.wait("tensor", s_gld)
        need("tensor", "m2psfree")
        need("vector", "hall_free")
        for ft in range(NFT):
            gbuf, ubuf = wg_sl[ft % NBUF], wu_sl[ft % NBUF]
            em.wait("tensor", s_ld, gu_ready[ft])
            em.wait("tensor", s_dve, slab_psum_free.get(ft % 2, 0))
            em.wait("tensor", s_act, slab_psum_free_a.get(ft % 2, 0))
            psG = psC[:, :] if ft % 2 == 0 else psA[:, 0:512]
            psU = psD[:, :] if ft % 2 == 0 else psA[:, 512:1024]
            psR = psE[:, :] if ft % 2 == 0 else psB[:, 0:512]
            for dt in range(ND):
                em.op("tensor", lambda e, dt=dt, psG=psG, b=gbuf: e.matmul(
                    psG, b[:, dt * 128:(dt + 1) * 128], xn_tgb[:, dt * TGT:(dt + 1) * TGT],
                    start=(dt == 0), stop=(dt == ND - 1)))
            for dt in range(ND):
                em.op("tensor", lambda e, dt=dt, psU=psU, b=ubuf: e.matmul(
                    psU, b[:, dt * 128:(dt + 1) * 128], xn_tgb[:, dt * TGT:(dt + 1) * TGT],
                    start=(dt == 0), stop=(dt == ND - 1)), inc=(s_pe if dt == ND - 1 else None))
            routed = ft >= 8
            if routed:
                ex = (ft - 8) // 2
                em.op("tensor", lambda e, ex=ex, psR=psR, tg=tg: e.matmul(
                    psR, C["sel16"][:, ex * 128:(ex + 1) * 128], combT[:, tg * TGT:(tg + 1) * TGT],
                    start=True, stop=True), inc=s_pe)
            slab_last_use[("gu", ft % NBUF)] = s_pe.n
            em.wait("scalar", s_pe)
            em.wait("scalar", s_dve)
            em.op("scalar", lambda e, psG=psG: e.activation(htmp[:, :], psG, AF.Silu))
            em.wait("vector", s_act)
            dsth = h_all[:, ft * TGT:(ft + 1) * TGT]
            em.op("vector", lambda e, psU=psU, dsth=dsth: e.tensor_tensor(dsth, htmp[:, :], psU, ALU.mult))
            if routed:
                em.op("vector", lambda e, psR=psR, dsth=dsth: e.tensor_tensor(dsth, dsth, psR, ALU.mult))
            slab_psum_free[ft % 2] = s_dve.n
            slab_psum_free_a[ft % 2] = s_act.n
            if ft + NBUF < NFT:
                issue_gu(ft + NBUF)
        mark("xntgb_free", s_pe)
        em.wait("tensor", s_dve)
        em.wait("tensor", s_act)
        wd_ready = {}
        def issue_wd(kt):
            bi = kt % 2
            lu = slab_last_use.get(("wd", bi))
            if lu is not None:
                em.wait("sync", s_pe, lu)
            em.wait("sync", s_pk, (2 * NFT + kt + 1) * 16)
            em.op("sync", lambda e, kt=kt, b=wd_sl[bi]: e.dma_start(out=b[:, :], in_=wd_t[kt, :, :]), inc=s_ld, inck=16)
            wd_ready[kt] = s_ld.n
        issue_wd(0); issue_wd(1)
        for kt in range(NFT):
            dbuf = wd_sl[kt % 2]
            em.wait("tensor", s_ld, wd_ready[kt])
            for dt in range(ND):
                em.op("tensor", lambda e, kt=kt, dt=dt, b=dbuf: e.matmul(
                    psm2[dt], b[:, dt * 128:(dt + 1) * 128], h_all[:, kt * TGT:(kt + 1) * TGT],
                    start=(kt == 0), stop=(kt == NFT - 1)),
                    inc=(s_pe if dt == ND - 1 else None))
            slab_last_use[("wd", kt % 2)] = s_pe.n
            if kt + 2 < NFT:
                issue_wd(kt + 2)
        mark("hall_free", s_pe)
        em.wait("scalar", s_pe)
        em.wait("scalar", s_st)
        for dt in range(ND):
            em.op("scalar", lambda e, dt=dt: e.activation(
                acc_tg[:, dt * TGT:(dt + 1) * TGT], psm2[dt],
                AF.Identity, bias=C["b2"][:, dt:dt + 1]), inc=(s_act if dt == ND - 1 else None))
        mark("m2psfree", s_act)
        em.wait("sync", s_act)
        em.op("sync", lambda e, tg=tg: e.dma_start(out=acc_hbm[tg, :, :], in_=acc_tg[:, :]), inc=s_st, inck=16)

    slab_psum_free = {}
    slab_psum_free_a = {}
    for tg in range(NTG):
        p3_tg(tg)
    em.join()
    em.flush()
    stM.close()

    stS = ExitStack()
    SBS = lambda name, shape, dt: stS.enter_context(nc.sbuf_tensor(name, list(shape), dt))
    o_heads = [SBS(f"o_h{h}", [128, TC], BF16) for h in range(H)]
    wo_sb = SBS("wo_sb", [128, ND * D], BF16)
    pstqs = SBS("pstqs", [128, 2 * H * 128], BF16)
    abuf = [SBS(f"abuf{i}", [128, H * 128], BF16) for i in range(2)]
    selb = SBS("selb", [128, H * 128], BF16)
    ztb = [SBS(f"ztb{i}", [128, H * 128], BF16) for i in range(2)]
    ptb = [SBS(f"ptb{i}", [128, H * 128], BF16) for i in range(2)]
    qab = [SBS(f"qab{i}", [128, H * 128], BF16) for i in range(2)]
    obb = [SBS(f"obb{i}", [128, H * 128], BF16) for i in range(2)]
    x1buf = SBS("x1buf", [128, ND * TGT], F32)
    sqbuf = SBS("sqbuf", [128, D], F32)
    sqbuf2 = SBS("sqbuf2", [128, D], F32)
    rowp2 = SBS("rowp2", [1, 512], F32)
    h2_tg = SBS("h2_tg", [128, ND * TGT], BF16)
    g1buf = SBS("g1buf", [128, (FMLP // 128) * TGT], BF16)
    acc_tg2 = SBS("acc_tg2", [128, ND * TGT], F32)
    w1_sl = [SBS(f"w1_sl{i}", [128, ND * 128], BF16) for i in range(2)]
    w2_sl = [SBS(f"w2_sl{i}", [128, ND * 128], BF16) for i in range(2)]
    rowp = SBS("rowp", [1, 512], F32)

    em.op("gpsimd", lambda e: e.dma_start(
        out=wo_sb[:, :].rearrange("p (k m) -> p k m", k=ND),
        in_=dram["wo"].rearrange("(k p) m -> p k m", p=128)), inc=s_gld, inck=16)

    em.wait("sync", s_cc)
    em.wait("vector", s_cc)
    em.wait("gpsimd", s_cc)
    em.op("vector", lambda e: e.memset(s_slab[:, :], 0.0))
    cur = 0
    for r in range(NCORES - 1):
        em.wait("sync", s_pe)
        em.wait("sync", s_dve)
        em.op("sync", lambda e, r=r: e.dma_start(out=pstqs[:, :], in_=ag_out[r * 128:(r + 1) * 128, :]),
              inc=s_ld, inck=16)
        em.wait("vector", s_ld)
        em.wait("tensor", s_ld)
        if r == 0:
            em.op("vector", lambda e: e.tensor_copy(abuf[0][:, :], pstqs[:, 1024:2048]))
        else:
            em.wait("tensor", s_dve)
            for h in range(H):
                em.op("tensor", lambda e, h=h, cur=cur: e.matmul(
                    psA[:, h * 128:(h + 1) * 128], pstqs[:, h * 128:(h + 1) * 128],
                    abuf[cur][:, h * 128:(h + 1) * 128], start=True, stop=True),
                    inc=(s_pe if h == H - 1 else None))
            em.wait("vector", s_pe)
            em.op("vector", lambda e, cur=cur: e.tensor_tensor(
                abuf[1 - cur][:, :], pstqs[:, 1024:2048], psA[:, :], ALU.add))
            cur = 1 - cur
        em.op("vector", lambda e, r=r, cur=cur: e.tensor_scalar(
            selb[:, :], abuf[cur][:, :], C["coresel"][:, r:r + 1], None, ALU.mult))
        em.op("vector", lambda e: e.tensor_tensor(s_slab[:, :], s_slab[:, :], selb[:, :], ALU.add))
    em.join()
    em.join()
    em.flush()

    seq_last_use = {}
    for c in range(NCH):
        pb = c % 2
        lu = seq_last_use.get(pb)
        if lu is not None:
            em.wait("sync", s_dve, lu)
        for nm, bufs, hbm in (("zt", ztb, zt_hbm), ("pt", ptb, pt_hbm),
                              ("qa", qab, qa_hbm), ("ob", obb, obase_hbm)):
            em.op("sync", lambda e, b=bufs[pb], hb=hbm, c=c: e.dma_start(out=b[:, :], in_=hb[c, :, :]),
                  inc=s_ld, inck=16)
        em.wait("tensor", s_ld)
        em.join(("tensor",))
        for h in range(H):
            em.op("tensor", lambda e, h=h, pb=pb: e.matmul(
                psA[:, h * 128:(h + 1) * 128], s_slab[:, h * 128:(h + 1) * 128],
                ztb[pb][:, h * 128:(h + 1) * 128], start=True, stop=True))
        for h in range(H):
            em.op("tensor", lambda e, h=h, pb=pb: e.matmul(
                psB[:, h * 128:(h + 1) * 128], ptb[pb][:, h * 128:(h + 1) * 128],
                s_slab[:, h * 128:(h + 1) * 128], start=True, stop=True),
                inc=(s_pe if h == H - 1 else None))
        em.wait("vector", s_pe)
        for h in range(H):
            em.op("vector", lambda e, h=h, pb=pb, c=c: e.tensor_tensor(
                o_heads[h][:, c * 128:(c + 1) * 128], psA[:, h * 128:(h + 1) * 128],
                obb[pb][:, h * 128:(h + 1) * 128], ALU.add))
        em.op("vector", lambda e, pb=pb: e.tensor_tensor(s_slab[:, :], psB[:, :], qab[pb][:, :], ALU.add))
        seq_last_use[pb] = s_dve.n
    em.flush()

    psm2b = [psA[:, 0:512], psA[:, 512:1024], psB[:, 0:512], psB[:, 512:1024],
             psC[:, :], psD[:, :], psE[:, :], psF[:, :]]
    mlp_last_use = {}
    sqbufs = [sqbuf, sqbuf2]
    rowps = [rowp, rowp2]
    ln2_bank_free = {}
    ln2_sq_free = {}
    ln2_rp_free = {}
    wo_ps_free = {}
    for tg in range(NTG):
        em.wait("sync", s_dve)
        em.wait("sync", s_st)
        for ci in range(4):
            c = tg * 4 + ci
            em.op("sync", lambda e, c=c, ci=ci: e.dma_start(
                out=x1buf[:, :].rearrange("p (d f t) -> p d f t", d=ND, f=4)[:, :, ci, :],
                in_=xn_hbm[c].rearrange("p (d t) -> p d t", d=ND)), inc=s_ld, inck=16)
        em.op("sync", lambda e, tg=tg: e.dma_start(out=acc_tg2[:, :], in_=acc_hbm[tg, :, :]), inc=s_ld, inck=16)
        em.wait("tensor", s_gld)
        em.wait("tensor", s_ld)
        em.wait("vector", s_ld)
        em.wait("tensor", s_dve)
        em.wait("tensor", s_act)
        for dt in range(ND):
            ps = psC if dt % 2 == 0 else psD
            if dt % 2 in wo_ps_free:
                em.wait("tensor", s_dve, wo_ps_free[dt % 2])
            for kt in range(ND):
                em.op("tensor", lambda e, dt=dt, kt=kt, ps=ps, tg=tg: e.matmul(
                    ps[:, :], wo_sb[:, kt * D + dt * 128: kt * D + (dt + 1) * 128],
                    o_heads[kt][:, tg * TGT:(tg + 1) * TGT],
                    start=(kt == 0), stop=(kt == ND - 1)), inc=(s_pe if kt == ND - 1 else None))
            em.wait("vector", s_pe)
            em.op("vector", lambda e, dt=dt, ps=ps: e.tensor_tensor(
                x1buf[:, dt * TGT:(dt + 1) * TGT], x1buf[:, dt * TGT:(dt + 1) * TGT], ps[:, :], ALU.add))
            em.op("vector", lambda e, dt=dt, ps=ps: e.tensor_tensor(
                acc_tg2[:, dt * TGT:(dt + 1) * TGT],
                acc_tg2[:, dt * TGT:(dt + 1) * TGT], ps[:, :], ALU.add), inc=s_dve)
            wo_ps_free[dt % 2] = s_dve.n
        dve_wo = s_dve.n
        for ci in range(4):
            par2 = ci % 2
            bank = psE if par2 == 0 else psF
            sqb = sqbufs[par2]
            rp = rowps[par2]
            def x1c(dt, ci=ci):
                return x1buf[:, dt * TGT + ci * 128: dt * TGT + (ci + 1) * 128]
            em.wait("vector", s_dve, dve_wo)
            if par2 in ln2_sq_free:
                em.wait("vector", s_act, ln2_sq_free[par2])
            for dt in range(ND):
                em.op("vector", lambda e, dt=dt, x1c=x1c, sqb=sqb: e.tensor_tensor(
                    sqb[:, dt * 128:(dt + 1) * 128], x1c(dt), x1c(dt), ALU.mult),
                    inc=(s_dve if dt == ND - 1 else None))
            dve_sq = s_dve.n
            em.wait("tensor", s_dve, dve_sq)
            if par2 in ln2_bank_free:
                em.wait("tensor", s_dve, ln2_bank_free[par2])
            for dt in range(ND):
                em.op("tensor", lambda e, dt=dt, x1c=x1c, bank=bank: e.matmul(
                    bank[0:1, 0:128], C["onesf_col"][:, :], x1c(dt),
                    start=(dt == 0), stop=(dt == ND - 1)))
            for dt in range(ND):
                em.op("tensor", lambda e, dt=dt, sqb=sqb, bank=bank: e.matmul(
                    bank[0:1, 128:256], C["onesf_col"][:, :], sqb[:, dt * 128:(dt + 1) * 128],
                    start=(dt == 0), stop=(dt == ND - 1)), inc=(s_pe if dt == ND - 1 else None))
            pe_sums = s_pe.n
            em.wait("vector", s_pe, pe_sums)
            if par2 in ln2_rp_free:
                em.wait("vector", s_pe, ln2_rp_free[par2])
            em.op("vector", lambda e, rp=rp, bank=bank: e.tensor_copy(rp[0:1, 0:256], bank[0:1, 0:256]))
            em.op("vector", lambda e, rp=rp: e.tensor_scalar_mul(rp[0:1, 0:128], rp[0:1, 0:128], 1.0 / D))
            em.op("vector", lambda e, rp=rp: e.tensor_scalar_mul(rp[0:1, 128:256], rp[0:1, 128:256], 1.0 / D))
            em.op("vector", lambda e, rp=rp: e.tensor_tensor(rp[0:1, 256:384], rp[0:1, 0:128], rp[0:1, 0:128], ALU.mult))
            em.op("vector", lambda e, rp=rp: e.tensor_tensor(rp[0:1, 128:256], rp[0:1, 128:256], rp[0:1, 256:384], ALU.subtract))
            em.op("vector", lambda e, rp=rp: e.tensor_scalar_add(rp[0:1, 128:256], rp[0:1, 128:256], EPS))
            em.op("vector", lambda e, rp=rp: e.reciprocal(rp[0:1, 128:256], rp[0:1, 128:256]), inc=s_dve)
            em.wait("scalar", s_dve)
            em.op("scalar", lambda e, rp=rp: e.activation(rp[0:1, 128:256], rp[0:1, 128:256], AF.Sqrt), inc=s_act)
            em.wait("tensor", s_act)
            em.op("tensor", lambda e, rp=rp, bank=bank: e.matmul(bank[:, 256:384], C["onesf_row"][:, :], rp[0:1, 0:128],
                                                                start=True, stop=True))
            em.op("tensor", lambda e, rp=rp, bank=bank: e.matmul(bank[:, 384:512], C["onesf_row"][:, :], rp[0:1, 128:256],
                                                                start=True, stop=True), inc=s_pe)
            pe_reps = s_pe.n
            ln2_rp_free[par2] = pe_reps
            em.wait("vector", s_pe, pe_reps)
            for dt in range(ND):
                em.op("vector", lambda e, dt=dt, x1c=x1c, sqb=sqb, bank=bank: e.tensor_tensor(
                    sqb[:, dt * 128:(dt + 1) * 128], x1c(dt), bank[:, 256:384], ALU.subtract))
            for dt in range(ND):
                em.op("vector", lambda e, dt=dt, sqb=sqb, bank=bank: e.tensor_tensor(
                    sqb[:, dt * 128:(dt + 1) * 128], sqb[:, dt * 128:(dt + 1) * 128],
                    bank[:, 384:512], ALU.mult), inc=(s_dve if dt == ND - 1 else None))
            ln2_bank_free[par2] = s_dve.n
            em.wait("scalar", s_dve)
            need("scalar", "h2_free")
            for dt in range(ND):
                em.op("scalar", lambda e, dt=dt, ci=ci, sqb=sqb: e.activation(
                    h2_tg[:, dt * TGT + ci * 128: dt * TGT + (ci + 1) * 128],
                    sqb[:, dt * 128:(dt + 1) * 128], AF.Identity,
                    scale=C["ln2w"][:, dt:dt + 1], bias=C["ln2b"][:, dt:dt + 1]),
                    inc=(s_act if dt == ND - 1 else None))
            ln2_sq_free[par2] = s_act.n
        em.wait("tensor", s_act)
        em.wait("tensor", s_dve)
        w1_ready = {}
        def issue_w1(ft):
            bi = ft % 2
            lu = mlp_last_use.get(("w1", bi))
            if lu is not None:
                em.wait("sync", s_pe, lu)
            em.wait("sync", s_pk, (3 * NFT + ft + 1) * 16)
            em.op("sync", lambda e, ft=ft, b=w1_sl[bi]: e.dma_start(out=b[:, :], in_=w1_t[ft, :, :]), inc=s_ld, inck=16)
            w1_ready[ft] = s_ld.n
        issue_w1(0); issue_w1(1)
        for ft in range(FMLP // 128):
            wb = w1_sl[ft % 2]
            ps = psC if ft % 2 == 0 else psD
            em.wait("tensor", s_ld, w1_ready[ft])
            em.wait("tensor", s_act, mlp_last_use.get(("psg", ft % 2), 0))
            for dt in range(ND):
                em.op("tensor", lambda e, dt=dt, ps=ps, b=wb: e.matmul(
                    ps[:, :], b[:, dt * 128:(dt + 1) * 128], h2_tg[:, dt * TGT:(dt + 1) * TGT],
                    start=(dt == 0), stop=(dt == ND - 1)), inc=(s_pe if dt == ND - 1 else None))
            mlp_last_use[("w1", ft % 2)] = s_pe.n
            em.wait("scalar", s_pe)
            em.op("scalar", lambda e, ft=ft, ps=ps: e.activation(
                g1buf[:, ft * TGT:(ft + 1) * TGT], ps[:, :], AF.Gelu,
                bias=C["b1"][:, ft:ft + 1]))
            mlp_last_use[("psg", ft % 2)] = s_act.n
            if ft + 2 < FMLP // 128:
                issue_w1(ft + 2)
        mark("h2_free", s_pe)
        em.join(("tensor",))
        w2_ready = {}
        def issue_w2(kt):
            bi = kt % 2
            lu = mlp_last_use.get(("w2", bi))
            if lu is not None:
                em.wait("sync", s_pe, lu)
            em.wait("sync", s_pk, (3 * NFT + FMLP // 128 + kt + 1) * 16)
            em.op("sync", lambda e, kt=kt, b=w2_sl[bi]: e.dma_start(out=b[:, :], in_=w2_t[kt, :, :]), inc=s_ld, inck=16)
            w2_ready[kt] = s_ld.n
        issue_w2(0); issue_w2(1)
        for kt in range(FMLP // 128):
            wb = w2_sl[kt % 2]
            em.wait("tensor", s_ld, w2_ready[kt])
            for dt in range(ND):
                em.op("tensor", lambda e, kt=kt, dt=dt, b=wb: e.matmul(
                    psm2b[dt], b[:, dt * 128:(dt + 1) * 128], g1buf[:, kt * TGT:(kt + 1) * TGT],
                    start=(kt == 0), stop=(kt == FMLP // 128 - 1)),
                    inc=(s_pe if dt == ND - 1 else None))
            mlp_last_use[("w2", kt % 2)] = s_pe.n
            if kt + 2 < FMLP // 128:
                issue_w2(kt + 2)
        em.wait("vector", s_pe)
        for dt in range(ND):
            em.op("vector", lambda e, dt=dt: e.tensor_tensor(
                acc_tg2[:, dt * TGT:(dt + 1) * TGT],
                acc_tg2[:, dt * TGT:(dt + 1) * TGT], psm2b[dt], ALU.add))
        em.join(("sync",))
        em.op("sync", lambda e, tg=tg: e.dma_start(out=acc_hbm[tg, :, :], in_=acc_tg2[:, :]), inc=s_st, inck=16)
        em.join()
        em.flush()
    stS.close()

    stF = ExitStack()
    SBF = lambda name, shape, dt: stF.enter_context(nc.sbuf_tensor(name, list(shape), dt))
    accf = SBF("accf", [128, D], F32)
    sumf = SBF("sumf", [128, D], F32)
    outtm = SBF("outtm", [128, D], F32)
    acc_tg3 = SBF("acc_tg3", [128, ND * TGT], F32)
    for tg in range(NTG):
        em.wait("sync", s_dve)
        em.op("sync", lambda e, tg=tg: e.dma_start(out=acc_tg3[:, :], in_=acc_hbm[tg, :, :]), inc=s_ld, inck=16)
        for ci in range(4):
            c = tg * 4 + ci
            em.wait("sync", s_dve)
            em.op("sync", lambda e, c=c: e.dma_start(out=accf[:, :], in_=base_hbm[c, :, :]), inc=s_ld, inck=16)
            em.wait("vector", s_ld)
            em.op("vector", lambda e, ci=ci: e.tensor_tensor(
                sumf[:, :].rearrange("p (d t) -> p d t", d=ND),
                acc_tg3[:, :].rearrange("p (d t) -> p d t", d=ND)[:, :, ci * 128:(ci + 1) * 128],
                accf[:, :].rearrange("p (d t) -> p d t", d=ND), ALU.add), inc=s_dve)
            em.wait("tensor", s_dve)
            for dt in range(ND):
                em.op("tensor", lambda e, dt=dt: e.transpose(
                    psA[:, dt * 128:(dt + 1) * 128], sumf[:, dt * 128:(dt + 1) * 128], C["identf"][:, :]),
                    inc=(s_pe if dt == ND - 1 else None))
            em.wait("vector", s_pe)
            em.op("vector", lambda e: e.tensor_copy(outtm[:, :], psA[:, :]), inc=s_dve)
            em.wait("sync", s_dve)
            em.op("sync", lambda e, c=c: e.dma_start(out=y[c * 128:(c + 1) * 128, :], in_=outtm[:, :]), inc=s_st, inck=16)
    em.flush()
    stF.close()

    em.wait("sync", s_st)
    em.flush()
    stB.close()
    st.close()
    return nc


def _build_consts():
    ar = np.arange(128)
    consts = {}
    consts["identf"] = np.eye(128, dtype=np.float32)
    consts["identb"] = np.eye(128, dtype=np.float32)
    consts["utri"] = (ar[:, None] <= ar[None, :]).astype(np.float32)
    lr = np.zeros((128, 128), np.float32); lr[127, :] = 1.0
    consts["lastrow"] = lr
    sel = np.zeros((16, 16 * 128), np.float32)
    for j in range(16):
        sel[j, j * 128:(j + 1) * 128] = 1.0
    consts["sel16"] = sel
    lt = (ar[None, :] < ar[:, None]).astype(np.float32)
    ut = (ar[:, None] < ar[None, :]).astype(np.float32)
    ui = (ar[:, None] <= ar[None, :]).astype(np.float32)
    consts["ltri_sneg"] = np.tile(-1000.0 * (1.0 - lt), (1, H))
    consts["utri_sneg"] = np.tile(-ut, (1, H))
    consts["utri_incl"] = np.tile(-1000.0 * (1.0 - ui), (1, H))
    consts["onesf_row"] = np.ones((1, 128), np.float32)
    consts["onesf_col"] = np.ones((128, 1), np.float32)
    return consts


_NC_CACHE = {}


def kernel(L0_attn_x, L0_attn_Wq, L0_attn_Wk, L0_attn_Wv, L0_attn_Wb, L0_attn_Wa,
           L0_attn_Wo, L0_attn_ln1_w, L0_attn_ln1_b, L0_attn_ln2_w, L0_attn_ln2_b,
           L0_attn_W1, L0_attn_b1, L0_attn_W2, L0_attn_b2, L0_ffn_router_w,
           L0_ffn_wg_r, L0_ffn_wu_r, L0_ffn_wd_r, L0_ffn_wg_s, L0_ffn_wu_s,
           L0_ffn_wd_s):
    import ml_dtypes

    x = np.ascontiguousarray(np.asarray(L0_attn_x, np.float32).reshape(16384, 1024))
    f32 = lambda a: np.ascontiguousarray(np.asarray(a, np.float32))

    def colmaj(vec):
        return np.ascontiguousarray(f32(vec).reshape(-1, 128).T)

    consts = _build_consts()
    wab = np.ascontiguousarray(np.concatenate([f32(L0_attn_Wa), f32(L0_attn_Wb)], axis=1))
    wg_s = f32(L0_ffn_wg_s).transpose(1, 0, 2).reshape(1024, NSH * MIDS)
    wg_r = f32(L0_ffn_wg_r).transpose(1, 0, 2).reshape(1024, NE * MIDR)
    wu_s = f32(L0_ffn_wu_s).transpose(1, 0, 2).reshape(1024, NSH * MIDS)
    wu_r = f32(L0_ffn_wu_r).transpose(1, 0, 2).reshape(1024, NE * MIDR)
    wgu = np.ascontiguousarray(np.concatenate([wg_s, wg_r, wu_s, wu_r], axis=1))
    wd = np.ascontiguousarray(np.concatenate(
        [f32(L0_ffn_wd_s).reshape(NSH * MIDS, 1024), f32(L0_ffn_wd_r).reshape(NE * MIDR, 1024)], axis=0))

    if "nc" not in _NC_CACHE:
        _NC_CACHE["nc"] = build_nc()
    nc = _NC_CACHE["nc"]

    in_maps = []
    for core in range(NCORES):
        cs = np.zeros((128, NCORES), np.float32)
        if core > 0:
            cs[:, core - 1] = 1.0
        m = {
            "x": x[core * TC:(core + 1) * TC],
            "wq": f32(L0_attn_Wq), "wk": f32(L0_attn_Wk), "wv": f32(L0_attn_Wv),
            "wab": wab, "wo": f32(L0_attn_Wo),
            "ln1w": colmaj(L0_attn_ln1_w), "ln1b": colmaj(L0_attn_ln1_b),
            "ln2w": colmaj(L0_attn_ln2_w), "ln2b": colmaj(L0_attn_ln2_b),
            "w1": f32(L0_attn_W1), "b1": colmaj(L0_attn_b1),
            "w2": f32(L0_attn_W2), "b2": colmaj(L0_attn_b2),
            "rw": f32(L0_ffn_router_w), "wgu": wgu, "wd": wd,
            "identb": consts["identb"].astype(ml_dtypes.bfloat16),
            "identb_rep": np.tile(consts["identb"], (1, H)).astype(ml_dtypes.bfloat16),
            "coresel": cs,
        }
        for nm in ("identf", "utri", "lastrow", "sel16", "ltri_sneg",
                   "utri_sneg", "utri_incl", "onesf_row", "onesf_col"):
            m[nm] = consts[nm]
        in_maps.append(m)

    res = run_bass_kernel_spmd(nc, in_maps, core_ids=list(range(NCORES)))

    out = np.concatenate([np.asarray(res.results[c]["y"]) for c in range(NCORES)], axis=0)
    return out.reshape(1, 16384, 1024).astype(np.float32)


# version 22
# speedup vs baseline: 1.0149x; 1.0149x over previous
import numpy as np
from contextlib import ExitStack

import concourse.bass as bass
import concourse.mybir as mybir
from concourse.bass_utils import run_bass_kernel_spmd

F32 = mybir.dt.float32
BF16 = mybir.dt.bfloat16
AF = mybir.ActivationFunctionType
ALU = mybir.AluOpType
AX = mybir.AxisListType.X

NCORES = 8
TC = 2048
CH = 128
NCH = TC // CH
TGT = 512
NTG = TC // TGT
D = 1024
ND = D // 128
H = 8
NE = 16
MIDR = 256
NSH = 2
MIDS = 512
NMID = NSH * MIDS + NE * MIDR
NFT = NMID // 128
FMLP = 4096
EPS = 1e-5

ENGINES = ("sync", "tensor", "vector", "scalar", "gpsimd")


class Sem:
    __slots__ = ("h", "n")

    def __init__(self, h):
        self.h = h
        self.n = 0


class Em:
    def __init__(self, nc):
        self.nc = nc
        self.q = {k: [] for k in ENGINES}
        self.lastwait = {}
        self.sems = []
        self.autosem = {}

    def wait(self, eng, sem, val=None):
        if val is None:
            val = sem.n
        if val <= 0:
            return
        key = (eng, id(sem))
        if self.lastwait.get(key, -1) >= val:
            return
        self.lastwait[key] = val
        self.q[eng].append(lambda e, h=sem.h, v=val: e.wait_ge(h, v))

    def join(self, engs=ENGINES):
        for eng in engs:
            for s in self.sems:
                self.wait(eng, s)

    def op(self, eng, fn, inc=None, inck=1):
        auto = self.autosem.get(eng)
        if auto is not None:
            self.wait(eng, auto)
            inc, inck = auto, 1
        if inc is not None:
            inc.n += inck
            self.q[eng].append(lambda e, f=fn, h=inc.h, k=inck: f(e).then_inc(h, k))
        else:
            self.q[eng].append(lambda e, f=fn: f(e))

    def flush(self):
        if not any(self.q.values()):
            return
        with self.nc.Block() as blk:
            for name in ENGINES:
                fns = self.q[name]
                if not fns:
                    continue

                def body(e, fns=fns):
                    for f in fns:
                        f(e)
                getattr(blk, name)(body)
        self.q = {k: [] for k in ENGINES}
        self.lastwait = {}


def build_nc():
    nc = bass.Bass("TRN2", target_bir_lowering=False, debug=False)
    dram = {}

    def din(name, shape, dt=F32):
        dram[name] = nc.dram_tensor(name, list(shape), dt, kind="ExternalInput").ap()
        return dram[name]

    def dout(name, shape, dt=F32):
        dram[name] = nc.dram_tensor(name, list(shape), dt, kind="ExternalOutput").ap()
        return dram[name]

    def dscr(name, shape, dt):
        dram[name] = nc.dram_tensor(name, list(shape), dt, kind="Internal").ap()
        return dram[name]

    x_in = din("x", [TC, D])
    for nm, sh in [("wq", [D, D]), ("wk", [D, D]), ("wv", [D, D]), ("wab", [D, 16]),
                   ("wo", [D, D]), ("w1", [D, FMLP]), ("w2", [FMLP, D]),
                   ("rw", [D, NE]), ("wgu", [D, 2 * NMID]), ("wd", [NMID, D]),
                   ("ln1w", [128, ND]), ("ln1b", [128, ND]), ("ln2w", [128, ND]),
                   ("ln2b", [128, ND]), ("b1", [128, FMLP // 128]), ("b2", [128, ND]),
                   ("identf", [128, 128]), ("utri", [128, 128]), ("lastrow", [128, 128]),
                   ("sel16", [16, 16 * 128]), ("ltri_sneg", [128, H * 128]),
                   ("utri_sneg", [128, H * 128]), ("utri_incl", [128, H * 128]),
                   ("onesf_row", [1, 128]), ("onesf_col", [128, 1]),
                   ("coresel", [128, NCORES])]:
        din(nm, sh)
    din("identb", [128, 128], BF16)
    din("identb_rep", [128, H * 128], BF16)
    y = dout("y", [TC, D])

    xn_hbm = dscr("xn_hbm", [NCH, 128, D], F32)
    base_hbm = dscr("base_hbm", [NCH, 128, D], F32)
    obase_hbm = dscr("obase_hbm", [NCH, 128, H * 128], BF16)
    zt_hbm = dscr("zt_hbm", [NCH, 128, H * 128], BF16)
    pt_hbm = dscr("pt_hbm", [NCH, 128, H * 128], BF16)
    qa_hbm = dscr("qa_hbm", [NCH, 128, H * 128], BF16)
    acc_hbm = dscr("acc_hbm", [NTG, 128, ND * TGT], F32)
    wgu_t = dscr("wgu_t", [2 * NFT, 128, ND * 128], BF16)
    wd_t = dscr("wd_t", [NFT, 128, ND * 128], BF16)
    w1_t = dscr("w1_t", [FMLP // 128, 128, ND * 128], BF16)
    w2_t = dscr("w2_t", [FMLP // 128, 128, ND * 128], BF16)
    ag_in = dscr("ag_in", [128, H * 256], BF16)
    ag_out = dscr("ag_out", [NCORES * 128, H * 256], BF16)

    st = ExitStack()
    SB = lambda name, shape, dt: st.enter_context(nc.sbuf_tensor(name, list(shape), dt))
    PSUM = lambda name, shape, dt: st.enter_context(nc.psum_tensor(name, list(shape), dt))

    em = Em(nc)

    def SEM(name):
        s = Sem(st.enter_context(nc.semaphore(name)))
        em.sems.append(s)
        return s

    s_ld = SEM("s_ld")
    s_gld = SEM("s_gld")
    s_pe = SEM("s_pe")
    s_dve = SEM("s_dve")
    s_act = SEM("s_act")
    s_st = SEM("s_st")
    s_cc = SEM("s_cc")
    s_pk = Sem(st.enter_context(nc.semaphore("s_pk")))
    em.autosem = {"vector": s_dve, "scalar": s_act}

    C = {}
    for nm, sh, dt in [("identf", [128, 128], F32), ("identb", [128, 128], BF16),
                       ("utri", [128, 128], F32), ("lastrow", [128, 128], F32),
                       ("sel16", [16, 16 * 128], F32),
                       ("ltri_sneg", [128, H * 128], F32),
                       ("utri_sneg", [128, H * 128], F32),
                       ("utri_incl", [128, H * 128], F32),
                       ("onesf_row", [1, 128], F32), ("onesf_col", [128, 1], F32),
                       ("coresel", [128, NCORES], F32),
                       ("ln1w", [128, ND], F32), ("ln1b", [128, ND], F32),
                       ("ln2w", [128, ND], F32), ("ln2b", [128, ND], F32),
                       ("b1", [128, FMLP // 128], F32), ("b2", [128, ND], F32),
                       ("identb_rep", [128, H * 128], BF16)]:
        C[nm] = SB("c_" + nm, sh, dt)
    combT = SB("combT", [16, TC], F32)
    ps_slab = SB("ps_slab", [128, H * 128], BF16)
    qs_slab = SB("qs_slab", [128, H * 128], BF16)
    s_slab = SB("s_slab", [128, H * 128], BF16)

    psA = PSUM("psA", [128, 1024], F32)
    psB = PSUM("psB", [128, 1024], F32)
    psC = PSUM("psC", [128, 512], F32)
    psD = PSUM("psD", [128, 512], F32)
    psE = PSUM("psE", [128, 512], F32)
    psF = PSUM("psF", [128, 512], F32)

    for nm in C:
        em.op("sync", lambda e, d=C[nm], s=dram[nm]: e.dma_start(out=d[:, :], in_=s),
              inc=s_ld, inck=16)
    em.flush()

    stA = ExitStack()
    SBA = lambda name, shape, dt: stA.enter_context(nc.sbuf_tensor(name, list(shape), dt))

    wq_sb = SBA("wq_sb", [128, ND * D], BF16)
    wk_sb = SBA("wk_sb", [128, ND * D], BF16)
    wv_sb = SBA("wv_sb", [128, ND * D], BF16)
    wab_sb = SBA("wab_sb", [128, ND * 16], BF16)
    rw_sb = SBA("rw_sb", [128, ND * NE], F32)

    xtm = [SBA(f"xtm{i}", [128, D], F32) for i in range(2)]
    xnf = SBA("xnf", [128, D], F32)
    basef = SBA("basef", [128, D], F32)
    ntmp = SBA("ntmp", [128, D], F32)
    n_tg = SBA("n_tg", [128, ND * TGT], BF16)
    q_tg = SBA("q_tg", [128, H * TGT], BF16)
    k_tg = SBA("k_tg", [128, H * TGT], BF16)
    qs_f = None
    qs_f = ntmp[:, 0:TGT]
    q2_f = basef[:, 0:TGT]
    rowb = SBA("rowb", [1, TGT], F32)
    colb = SBA("colb", [128, 16], F32)
    logit = SBA("logit", [128, 16], F32)
    ptmp = SBA("ptmp", [128, 16], F32)
    ptmp2 = SBA("ptmp2", [128, 16], F32)
    ltT = SBA("ltT", [16, 128], F32)
    softc = SBA("softc", [128, 4], F32)
    PR = 2
    v_tm = [SBA(f"v_tm{p}", [128, H * 128], BF16) for p in range(PR)]
    k_tm = [SBA(f"k_tm{p}", [128, H * 128], BF16) for p in range(PR)]
    gk_tm = [SBA(f"gk_tm{p}", [128, H * 128], BF16) for p in range(PR)]
    kp_tm = [SBA(f"kp_tm{p}", [128, H * 128], BF16) for p in range(PR)]
    zab = [SBA(f"zab{p}", [128, 16], F32) for p in range(PR)]
    labuf = [SBA(f"labuf{p}", [128, 24], F32) for p in range(PR)]
    lamT = [SBA(f"lamT{p}", [16, 128], F32) for p in range(PR)]
    explamT = [SBA(f"explamT{p}", [16, 128], F32) for p in range(PR)]
    neglamT = [SBA(f"neglamT{p}", [16, 128], F32) for p in range(PR)]
    gamcol = [SBA(f"gamcol{p}", [128, 8], F32) for p in range(PR)]
    gccol = [SBA(f"gccol{p}", [128, 8], F32) for p in range(PR)]
    sccol = [SBA(f"sccol{p}", [128, 8], F32) for p in range(PR)]
    mbneg = [SBA(f"mbneg{p}", [128, 8], F32) for p in range(PR)]
    rfull = [SBA(f"rfull{p}", [128, H * 128], BF16) for p in range(PR)]
    rfullT = [SBA(f"rfullT{p}", [128, H * 128], BF16) for p in range(PR)]
    rfs = SBA("rfs", [128, H * 128], F32)
    ntsl = [SBA(f"ntsl{p}", [128, H * 128], BF16) for p in range(PR)]
    atsl = [SBA(f"atsl{p}", [128, H * 128], BF16) for p in range(PR)]
    chain = [[SBA(f"chain{p}_{i}", [128, H * 128], BF16) for i in range(2)] for p in range(PR)]
    nsl = [chain[p][0] for p in range(PR)]
    w_tm = [SBA(f"w_tm{p}", [128, H * 128], BF16) for p in range(PR)]
    u_tm = [SBA(f"u_tm{p}", [128, H * 128], BF16) for p in range(PR)]
    zt_sl = [SBA(f"zt_sl{p}", [128, H * 128], BF16) for p in range(PR)]
    pt_sl = [SBA(f"pt_sl{p}", [128, H * 128], BF16) for p in range(PR)]
    qa_sl = [SBA(f"qa_sl{p}", [128, H * 128], BF16) for p in range(PR)]
    zabT_tg = SBA("zabT_tg", [16, TGT], F32)

    RDY = {}

    def mark(nm, sem):
        RDY[nm] = (sem, sem.n)

    def need(eng, *nms):
        for nm in nms:
            if nm in RDY:
                s, v = RDY[nm]
                em.wait(eng, s, v)

    s_wl = Sem(st.enter_context(nc.semaphore("s_wl")))
    wstage = SBA("wstage", [128, D], F32)
    wsth = [wstage[:, 0:512], wstage[:, 512:1024]]
    for k in range(ND):
        em.op("sync", lambda e, k=k: e.dma_start(
            out=rw_sb[:, k * NE:(k + 1) * NE],
            in_=dram["rw"][k * 128:(k + 1) * 128, :]), inc=s_wl, inck=16)
    RDY["w_rw"] = (s_wl, s_wl.n)
    cast_jobs = []
    for nm, dst, m in [("wq", wq_sb, D), ("wk", wk_sb, D), ("wab", wab_sb, 16), ("wv", wv_sb, D)]:
        for k in range(ND):
            if m == D:
                cast_jobs.append((nm, dst, k, 0, 512))
                cast_jobs.append((nm, dst, k, 512, 512))
            else:
                cast_jobs.append((nm, dst, k, 0, m))
    cast_done = {}
    last = {}
    for j, (nm, dst, k, co, w) in enumerate(cast_jobs):
        sg = wsth[j % 2]
        if j >= 2:
            em.wait("sync", s_act, cast_done[j - 2])
        em.op("sync", lambda e, sg=sg, nm=nm, k=k, co=co, w=w: e.dma_start(
            out=sg[:, 0:w], in_=dram[nm][k * 128:(k + 1) * 128, co:co + w]),
            inc=s_wl, inck=16)
        em.wait("scalar", s_wl, s_wl.n)
        mw = D if w == 512 else w
        em.op("scalar", lambda e, sg=sg, dst=dst, k=k, co=co, w=w, mw=mw: e.activation(
            dst[:, k * mw + co:k * mw + co + w], sg[:, 0:w], AF.Identity))
        cast_done[j] = s_act.n
        last[nm] = True
        if j + 1 == len(cast_jobs) or cast_jobs[j + 1][0] != nm:
            mark("w_" + nm, s_act)
    pack_jobs = []
    for ft in range(NFT):
        pack_jobs.append(("c", dram["wgu"], ft * 128, wgu_t, ft))
        pack_jobs.append(("c", dram["wgu"], NMID + ft * 128, wgu_t, NFT + ft))
    for kt in range(NFT):
        pack_jobs.append(("r", dram["wd"], kt * 128, wd_t, kt))
    for ft in range(FMLP // 128):
        pack_jobs.append(("c", dram["w1"], ft * 128, w1_t, ft))
    for kt in range(FMLP // 128):
        pack_jobs.append(("r", dram["w2"], kt * 128, w2_t, kt))
    pack_cursor = [0]

    def emit_packs(n):
        lo = pack_cursor[0]
        hi = min(lo + n, len(pack_jobs))
        pack_cursor[0] = hi
        for kind, srcap, off, dstap, idx in pack_jobs[lo:hi]:
            if kind == "c":
                em.op("gpsimd", lambda e, srcap=srcap, off=off, dstap=dstap, idx=idx: e.dma_start(
                    out=dstap[idx, :, :],
                    in_=srcap[:, off:off + 128].rearrange("(k p) m -> p k m", p=128)),
                    inc=s_pk, inck=16)
            else:
                em.op("gpsimd", lambda e, srcap=srcap, off=off, dstap=dstap, idx=idx: e.dma_start(
                    out=dstap[idx, :, :], in_=srcap[off:off + 128, :]), inc=s_pk, inck=16)

    def p1_chunk(c):
        par = c % 2
        buf = xtm[par]
        BIG = psA if par == 0 else psB
        o = par * 8
        need("sync", f"xtmfree{par}")
        em.op("sync", lambda e, b=buf, c=c: e.dma_start(
            out=b[:, :], in_=x_in[c * 128:(c + 1) * 128, :]), inc=s_ld, inck=16)
        ld_x = s_ld.n
        em.wait("vector", s_ld, ld_x)
        need("vector", f"BIGfree{par}", f"colbfree{par}")
        em.op("vector", lambda e, b=buf, o=o: e.tensor_reduce(colb[:, o + 0:o + 1], b[:, :], AX, ALU.add))
        em.op("vector", lambda e, b=buf, BIG=BIG: e.tensor_tensor(BIG[:, :], b[:, :], b[:, :], ALU.mult))
        em.op("vector", lambda e, o=o, BIG=BIG: e.tensor_reduce(colb[:, o + 1:o + 2], BIG[:, :], AX, ALU.add))
        dve_x2 = s_dve.n
        em.op("vector", lambda e, o=o: e.tensor_scalar_mul(colb[:, o + 0:o + 1], colb[:, o + 0:o + 1], 1.0 / D))
        em.op("vector", lambda e, o=o: e.tensor_scalar_mul(colb[:, o + 1:o + 2], colb[:, o + 1:o + 2], 1.0 / D))
        em.op("vector", lambda e, o=o: e.tensor_tensor(colb[:, o + 2:o + 3], colb[:, o + 0:o + 1], colb[:, o + 0:o + 1], ALU.mult))
        em.op("vector", lambda e, o=o: e.tensor_tensor(colb[:, o + 2:o + 3], colb[:, o + 1:o + 2], colb[:, o + 2:o + 3], ALU.subtract))
        em.op("vector", lambda e, o=o: e.tensor_scalar_add(colb[:, o + 3:o + 4], colb[:, o + 2:o + 3], EPS))
        em.op("vector", lambda e, o=o: e.reciprocal(colb[:, o + 4:o + 5], colb[:, o + 3:o + 4]), inc=s_dve)
        dve_stats = s_dve.n
        em.wait("scalar", s_dve, dve_stats)
        em.op("scalar", lambda e, o=o: e.activation(colb[:, o + 5:o + 6], colb[:, o + 4:o + 5], AF.Sqrt))
        em.op("scalar", lambda e, o=o: e.activation(colb[:, o + 6:o + 7], colb[:, o + 2:o + 3], AF.Identity,
                                                    scale=colb[:, o + 4:o + 5]), inc=s_act)
        act_r1 = s_act.n
        em.wait("vector", s_act, act_r1)
        em.op("vector", lambda e, o=o: e.tensor_scalar_add(colb[:, o + 6:o + 7], colb[:, o + 6:o + 7], EPS))
        em.op("vector", lambda e, o=o: e.reciprocal(colb[:, o + 6:o + 7], colb[:, o + 6:o + 7]))
        em.op("vector", lambda e, o=o: e.scalar_tensor_tensor(colb[:, o + 7:o + 8], colb[:, o + 0:o + 1], -1.0,
                                                              colb[:, o + 5:o + 6], ALU.mult, ALU.mult), inc=s_dve)
        dve_s2 = s_dve.n
        em.wait("scalar", s_dve, dve_s2)
        em.op("scalar", lambda e, o=o: e.activation(colb[:, o + 6:o + 7], colb[:, o + 6:o + 7], AF.Sqrt))
        need("scalar", "ntmp_pe")
        em.op("scalar", lambda e, b=buf, o=o: e.activation(ntmp[:, :], b[:, :], AF.Identity,
                                                           scale=colb[:, o + 5:o + 6],
                                                           bias=colb[:, o + 7:o + 8]), inc=s_act)
        act_xn = s_act.n
        need("tensor", f"BIGfree{par}")
        em.wait("tensor", s_ld, ld_x)
        em.wait("tensor", s_dve, dve_x2)
        for dt in range(ND):
            em.op("tensor", lambda e, dt=dt, b=buf, BIG=BIG: e.transpose(
                BIG[:, dt * 128:(dt + 1) * 128], b[:, dt * 128:(dt + 1) * 128], C["identf"][:, :]),
                inc=(s_pe if dt == ND - 1 else None))
        pe_xT = s_pe.n
        em.wait("vector", s_pe, pe_xT)
        need("vector", "basef_free")
        em.op("vector", lambda e, BIG=BIG: e.tensor_copy(basef[:, :], BIG[:, :]), inc=s_dve)
        dve_bc = s_dve.n
        em.wait("tensor", s_act, act_xn)
        em.wait("tensor", s_dve, dve_bc)
        for dt in range(ND):
            em.op("tensor", lambda e, dt=dt, BIG=BIG: e.transpose(
                BIG[:, dt * 128:(dt + 1) * 128], ntmp[:, dt * 128:(dt + 1) * 128], C["identf"][:, :]),
                inc=(s_pe if dt == ND - 1 else None))
        pe_xnT = s_pe.n
        RDY["ntmp_pe"] = (s_pe, pe_xnT)
        em.wait("vector", s_pe, pe_xnT)
        need("vector", "xnf_free", "xnf_free2")
        em.op("vector", lambda e, BIG=BIG: e.tensor_copy(xnf[:, :], BIG[:, :]))
        em.op("vector", lambda e: e.tensor_tensor(basef[:, :], basef[:, :], xnf[:, :], ALU.add), inc=s_dve)
        dve_xnf = s_dve.n
        em.wait("sync", s_dve, dve_xnf)
        em.op("sync", lambda e, c=c: e.dma_start(out=xn_hbm[c, :, :], in_=xnf[:, :]), inc=s_st, inck=16)
        em.op("sync", lambda e, c=c: e.dma_start(out=base_hbm[c, :, :], in_=basef[:, :]), inc=s_st, inck=16)
        RDY["basef_free"] = (s_st, s_st.n)
        RDY["xnf_free2"] = (s_st, s_st.n)
        em.wait("scalar", s_pe, pe_xT)
        em.op("scalar", lambda e, b=buf, o=o: e.activation(b[:, :], ntmp[:, :], AF.Identity,
                                                           scale=colb[:, o + 6:o + 7]), inc=s_act)
        act_n = s_act.n
        RDY[f"colbfree{par}"] = (s_act, act_n)
        need("tensor", "w_rw", "ONEfree1")
        em.wait("tensor", s_dve, dve_xnf)
        for dt in range(ND):
            em.op("tensor", lambda e, dt=dt: e.matmul(
                psD[0:16, 0:128], rw_sb[:, dt * NE:(dt + 1) * NE], xnf[:, dt * 128:(dt + 1) * 128],
                start=(dt == 0), stop=(dt == ND - 1)), inc=(s_pe if dt == ND - 1 else None))
        pe_rt = s_pe.n
        RDY["xnf_free"] = (s_pe, pe_rt)
        em.wait("tensor", s_act, act_n)
        for dt in range(ND):
            em.op("tensor", lambda e, dt=dt, b=buf, BIG=BIG: e.transpose(
                BIG[:, dt * 128:(dt + 1) * 128], b[:, dt * 128:(dt + 1) * 128], C["identf"][:, :]),
                inc=(s_pe if dt == ND - 1 else None))
        pe_nT = s_pe.n
        RDY[f"xtmfree{par}"] = (s_pe, pe_nT)
        tgo = (c % 4) * 128
        em.wait("scalar", s_pe, pe_nT)
        need("scalar", "tgfree_pe")
        for dt in range(ND):
            em.op("scalar", lambda e, dt=dt, tgo=tgo, BIG=BIG: e.activation(
                n_tg[:, dt * TGT + tgo: dt * TGT + tgo + 128], BIG[:, dt * 128:(dt + 1) * 128],
                AF.Identity, scale=C["ln1w"][:, dt:dt + 1], bias=C["ln1b"][:, dt:dt + 1]),
                inc=(s_act if dt == ND - 1 else None))
        mark(f"n{c % 4}", s_act)
        mark(f"BIGfree{par}", s_act)
        em.wait("vector", s_pe, pe_rt)
        need("vector", "router_free")
        em.op("vector", lambda e: e.tensor_copy(ltT[:, :], psD[0:16, 0:128]), inc=s_dve)
        dve_lt = s_dve.n
        em.wait("tensor", s_dve, dve_lt)
        em.op("tensor", lambda e: e.transpose(psD[:, 128:144], ltT[:, :], C["identf"][0:16, 0:16]), inc=s_pe)
        pe_lt = s_pe.n
        em.wait("vector", s_pe, pe_lt)
        em.op("vector", lambda e: e.tensor_copy(logit[:, :], psD[:, 128:144]))
        em.op("vector", lambda e: e.tensor_reduce(softc[:, 0:1], logit[:, :], AX, ALU.max))
        em.op("vector", lambda e: e.tensor_scalar_mul(softc[:, 0:1], softc[:, 0:1], -1.0), inc=s_dve)
        em.wait("scalar", s_dve)
        em.op("scalar", lambda e: e.activation(ptmp[:, :], logit[:, :], AF.Exp, bias=softc[:, 0:1]), inc=s_act)
        em.wait("vector", s_act)
        em.op("vector", lambda e: e.tensor_reduce(softc[:, 1:2], ptmp[:, :], AX, ALU.add))
        em.op("vector", lambda e: e.reciprocal(softc[:, 1:2], softc[:, 1:2]))
        em.op("vector", lambda e: e.tensor_scalar(ptmp[:, :], ptmp[:, :], softc[:, 1:2], None, ALU.mult))
        em.op("vector", lambda e: e.tensor_reduce(softc[:, 0:1], ptmp[:, :], AX, ALU.max))
        em.op("vector", lambda e: e.tensor_scalar(ptmp2[:, :], ptmp[:, :], softc[:, 0:1], None, ALU.is_ge))
        em.op("vector", lambda e: e.scalar_tensor_tensor(ptmp2[:, :], ptmp2[:, :], -1e9, ptmp[:, :], ALU.mult, ALU.add))
        em.op("vector", lambda e: e.tensor_reduce(softc[:, 2:3], ptmp2[:, :], AX, ALU.max))
        em.op("vector", lambda e: e.tensor_scalar(ptmp2[:, :], ptmp[:, :], softc[:, 2:3], None, ALU.is_ge))
        em.op("vector", lambda e: e.tensor_tensor(ptmp2[:, :], ptmp2[:, :], ptmp[:, :], ALU.mult))
        em.op("vector", lambda e: e.tensor_tensor(softc[:, 3:4], softc[:, 0:1], softc[:, 2:3], ALU.add))
        em.op("vector", lambda e: e.tensor_scalar_add(softc[:, 3:4], softc[:, 3:4], 1e-9))
        em.op("vector", lambda e: e.reciprocal(softc[:, 3:4], softc[:, 3:4]))
        em.op("vector", lambda e: e.tensor_scalar(ptmp2[:, :], ptmp2[:, :], softc[:, 3:4], None, ALU.mult), inc=s_dve)
        dve_p2 = s_dve.n
        em.wait("tensor", s_dve, dve_p2)
        em.op("tensor", lambda e: e.transpose(psD[0:16, 256:384], ptmp2[:, :], C["identf"][:, :]), inc=s_pe)
        pe_p2 = s_pe.n
        RDY["router_free"] = (s_pe, pe_p2)
        em.wait("vector", s_pe, pe_p2)
        em.op("vector", lambda e, c=c: e.tensor_copy(combT[:, c * 128:(c + 1) * 128], psD[0:16, 256:384]), inc=s_dve)
        mark("ONEfree1", s_dve)
    qs2 = [ntmp[:, 0:512], ntmp[:, 512:1024]]
    q22 = [basef[:, 0:512], basef[:, 512:1024]]

    def p2_tg_proj(tg):
        need("tensor", "w_wq", "w_wk", "w_wab", "n0", "n1", "n2", "n3")
        need("tensor", "ONEfree0", "ONEfree1", "AUXfree0", "AUXfree1")
        need("vector", "tgfree_pe", "tgfree_dve", "tgfree_act")
        need("scalar", "tgfree_pe", "tgfree_dve", "tgfree_act")
        need("scalar", "ntmp_pe")
        need("vector", "basef_free")
        pb_free = {}
        qs_free = {}
        sum_free = 0
        rep_free = 0
        rowb_free = 0
        idx = 0
        for wsb, dst in ((wq_sb, q_tg), (wk_sb, k_tg)):
            for h in range(H):
                bp = idx % 2
                ps = psC if bp == 0 else psD
                if bp in pb_free:
                    em.wait("tensor", s_act, pb_free[bp])
                for dt in range(ND):
                    em.op("tensor", lambda e, dt=dt, h=h, wsb=wsb, ps=ps: e.matmul(
                        ps[:, :], wsb[:, dt * D + h * 128: dt * D + (h + 1) * 128],
                        n_tg[:, dt * TGT:(dt + 1) * TGT],
                        start=(dt == 0), stop=(dt == ND - 1)),
                        inc=(s_pe if dt == ND - 1 else None))
                pe_mm = s_pe.n
                em.wait("scalar", s_pe, pe_mm)
                if bp in qs_free:
                    em.wait("scalar", s_dve, qs_free[bp])
                em.op("scalar", lambda e, ps=ps, bp=bp: e.activation(qs2[bp][:, :], ps[:, :], AF.Silu),
                      inc=s_act)
                pb_free[bp] = s_act.n
                em.wait("vector", s_act, pb_free[bp])
                em.op("vector", lambda e, bp=bp: e.tensor_tensor(q22[bp][:, :], qs2[bp][:, :], qs2[bp][:, :], ALU.mult),
                      inc=s_dve)
                dve_sq = s_dve.n
                em.wait("tensor", s_dve, dve_sq)
                if sum_free:
                    em.wait("tensor", s_dve, sum_free)
                em.op("tensor", lambda e, bp=bp: e.matmul(psE[0:1, 0:TGT], C["onesf_col"][:, :], q22[bp][:, :],
                                                          start=True, stop=True), inc=s_pe)
                pe_sum = s_pe.n
                em.wait("vector", s_pe, pe_sum)
                if rowb_free:
                    em.wait("vector", s_pe, rowb_free)
                em.op("vector", lambda e: e.tensor_scalar_add(rowb[0:1, :], psE[0:1, 0:TGT], 1e-6))
                sum_free = s_dve.n
                em.op("vector", lambda e: e.reciprocal(rowb[0:1, :], rowb[0:1, :]), inc=s_dve)
                em.wait("scalar", s_dve)
                em.op("scalar", lambda e: e.activation(rowb[0:1, :], rowb[0:1, :], AF.Sqrt), inc=s_act)
                act_sqrt = s_act.n
                em.wait("tensor", s_act, act_sqrt)
                if rep_free:
                    em.wait("tensor", s_dve, rep_free)
                em.op("tensor", lambda e: e.matmul(psF[:, 0:TGT], C["onesf_row"][:, :], rowb[0:1, :],
                                                   start=True, stop=True), inc=s_pe)
                pe_rep = s_pe.n
                rowb_free = pe_rep
                em.wait("vector", s_pe, pe_rep)
                em.op("vector", lambda e, h=h, dst=dst, bp=bp: e.tensor_tensor(
                    dst[:, h * TGT:(h + 1) * TGT], qs2[bp][:, :], psF[:, 0:TGT], ALU.mult), inc=s_dve)
                qs_free[bp] = s_dve.n
                rep_free = s_dve.n
                idx += 1
        mark("ONEfree0", s_act)
        mark("ONEfree1", s_act)
        mark("AUXfree0", s_dve)
        RDY["ntmp_pe"] = (s_dve, qs_free[1])
        RDY["basef_free"] = (s_pe, pe_sum)
        em.wait("tensor", s_dve, rep_free)
        for dt in range(ND):
            em.op("tensor", lambda e, dt=dt: e.matmul(
                psF[0:16, 0:TGT], wab_sb[:, dt * 16:(dt + 1) * 16], n_tg[:, dt * TGT:(dt + 1) * TGT],
                start=(dt == 0), stop=(dt == ND - 1)), inc=(s_pe if dt == ND - 1 else None))
        pe_zab = s_pe.n
        em.wait("vector", s_pe, pe_zab)
        em.op("vector", lambda e: e.tensor_copy(zabT_tg[:, :], psF[0:16, 0:TGT]), inc=s_dve)
        mark("AUXfree1", s_dve)
        mark("qk", s_dve)

    def hsl(t, h):
        return t[:, h * 128:(h + 1) * 128]

    def st_v(c):
        pr = c % 2
        BIG = psA if pr == 0 else psB
        ONE = psC if pr == 0 else psD
        ci = c % 4
        nch = lambda dt: n_tg[:, dt * TGT + ci * 128: dt * TGT + (ci + 1) * 128]
        kch = lambda h: k_tg[:, h * TGT + ci * 128: h * TGT + (ci + 1) * 128]
        need("tensor", f"BIGfree{pr}", f"ONEfree{pr}", "qk", "w_wv", f"n{ci}")
        for half in range(2):
            for dt in range(ND):
                em.op("tensor", lambda e, dt=dt, half=half, BIG=BIG, nch=nch: e.matmul(
                    BIG[:, half * 512:(half + 1) * 512], nch(dt),
                    wv_sb[:, dt * D + half * 512: dt * D + (half + 1) * 512],
                    start=(dt == 0), stop=(dt == ND - 1)),
                    inc=(s_pe if dt == ND - 1 else None))
        pe_v = s_pe.n
        em.wait("scalar", s_pe, pe_v)
        need("scalar", f"vwar{pr}")
        em.op("scalar", lambda e, BIG=BIG, pr=pr: e.activation(v_tm[pr][:, 0:512], BIG[:, 0:512], AF.Silu))
        em.op("scalar", lambda e, BIG=BIG, pr=pr: e.activation(v_tm[pr][:, 512:1024], BIG[:, 512:1024], AF.Silu))
        mark(f"v{pr}", s_act)
        mark(f"BIGfree{pr}", s_act)
        ONEb = ONE[:, :].bitcast(BF16)
        need("tensor", f"ONEfree{pr}")
        for h in range(H):
            em.op("tensor", lambda e, h=h, ONEb=ONEb, kch=kch: e.transpose(
                hsl(ONEb, h), kch(h), C["identb"][:, :]), inc=(s_pe if h == H - 1 else None))
        em.wait("vector", s_pe)
        need("vector", f"ktmwar{pr}")
        em.op("vector", lambda e, ONEb=ONEb, pr=pr: e.tensor_copy(k_tm[pr][:, :], ONEb[:, :]))
        mark(f"ktm{pr}", s_dve)
        mark(f"ONEfree{pr}", s_dve)

    def st_la(c):
        pr = c % 2
        AUX = psE if pr == 0 else psF
        ci = c % 4
        la_, zab_, lamT_, explamT_, neglamT_ = labuf[pr], zab[pr], lamT[pr], explamT[pr], neglamT[pr]
        need("tensor", f"AUXfree{pr}", "qk")
        em.op("tensor", lambda e, AUX=AUX, ci=ci: e.transpose(
            AUX[:, 0:16], zabT_tg[0:16, ci * 128:(ci + 1) * 128],
            C["identf"][0:16, 0:16]), inc=s_pe)
        pe1 = s_pe.n
        em.wait("vector", s_pe, pe1)
        need("vector", f"lawar{pr}", f"rscal{pr}")
        em.op("vector", lambda e, AUX=AUX, zab_=zab_: e.tensor_copy(zab_[:, :], AUX[:, 0:16]))
        em.wait("scalar", s_dve)
        em.op("scalar", lambda e, la_=la_, zab_=zab_: e.activation(la_[:, 0:16], zab_[:, :], AF.Abs))
        em.op("scalar", lambda e, la_=la_: e.activation(la_[:, 0:16], la_[:, 0:16], AF.Exp, scale=-1.0))
        em.op("scalar", lambda e, la_=la_: e.activation(la_[:, 0:16], la_[:, 0:16], AF.Ln, bias=1.0))
        em.wait("vector", s_act)
        em.op("vector", lambda e, zab_=zab_: e.tensor_scalar(zab_[:, :], zab_[:, :], 0.0, None, ALU.min))
        em.op("vector", lambda e, la_=la_, zab_=zab_: e.tensor_tensor(la_[:, 0:16], zab_[:, :], la_[:, 0:16], ALU.subtract))
        dve1 = s_dve.n
        em.wait("tensor", s_dve, dve1)
        em.op("tensor", lambda e, AUX=AUX, la_=la_: e.matmul(AUX[:, 16:24], C["utri"][:, :], la_[:, 0:8],
                                                             start=True, stop=True), inc=s_pe)
        pe2 = s_pe.n
        em.wait("vector", s_pe, pe2)
        em.op("vector", lambda e, AUX=AUX, la_=la_: e.tensor_copy(la_[:, 16:24], AUX[:, 16:24]))
        em.op("vector", lambda e, la_=la_: e.tensor_tensor(la_[:, 8:16], la_[:, 16:24], la_[:, 8:16], ALU.subtract))
        em.op("vector", lambda e, la_=la_, pr=pr: e.tensor_scalar_mul(mbneg[pr][:, :], la_[:, 8:16], -1.0))
        dve2 = s_dve.n
        em.wait("tensor", s_dve, dve2)
        em.op("tensor", lambda e, AUX=AUX, la_=la_: e.matmul(AUX[:, 24:32], C["lastrow"][:, :], la_[:, 16:24],
                                                             start=True, stop=True))
        em.op("tensor", lambda e, AUX=AUX, la_=la_: e.transpose(AUX[0:16, 32:160], la_[:, 8:24], C["identf"][:, :]),
              inc=s_pe)
        pe3 = s_pe.n
        em.wait("vector", s_pe, pe3)
        em.op("vector", lambda e, AUX=AUX, lamT_=lamT_: e.tensor_copy(lamT_[:, :], AUX[0:16, 32:160]))
        em.op("vector", lambda e, AUX=AUX, pr=pr: e.tensor_copy(gccol[pr][:, :], AUX[:, 24:32]))
        em.op("vector", lambda e, lamT_=lamT_, neglamT_=neglamT_: e.tensor_scalar_mul(neglamT_[:, :], lamT_[:, :], -1.0))
        em.op("vector", lambda e, la_=la_, pr=pr: e.tensor_tensor(sccol[pr][:, :], gccol[pr][:, :], la_[:, 8:16], ALU.subtract))
        mark(f"AUXfree{pr}", s_dve)
        em.wait("scalar", s_dve)
        em.op("scalar", lambda e, lamT_=lamT_, explamT_=explamT_: e.activation(explamT_[:, :], lamT_[:, :], AF.Exp))
        em.op("scalar", lambda e, la_=la_, pr=pr: e.activation(gamcol[pr][:, :], la_[:, 16:24], AF.Exp))
        em.op("scalar", lambda e, pr=pr: e.activation(sccol[pr][:, :], sccol[pr][:, :], AF.Exp))
        em.op("scalar", lambda e, pr=pr: e.activation(gccol[pr][:, :], gccol[pr][:, :], AF.Exp))
        sc1 = s_act.n
        em.wait("vector", s_act, sc1)
        need("vector", f"ktm{pr}")
        for h in range(H):
            em.op("vector", lambda e, h=h, pr=pr: e.tensor_scalar(
                hsl(gk_tm[pr], h), hsl(k_tm[pr], h), gamcol[pr][:, h:h + 1], None, ALU.mult))
        for h in range(H):
            em.op("vector", lambda e, h=h, pr=pr: e.tensor_scalar(
                hsl(kp_tm[pr], h), hsl(k_tm[pr], h), sccol[pr][:, h:h + 1], None, ALU.mult))
        mark(f"la{pr}", s_dve)
        mark(f"gkkp{pr}", s_dve)

    def st_R(c):
        pr = c % 2
        BIG = psA if pr == 0 else psB
        need("tensor", f"BIGfree{pr}", f"la{pr}")
        for h in range(H):
            em.op("tensor", lambda e, h=h, BIG=BIG, pr=pr: e.matmul(
                hsl(BIG, h), C["sel16"][:, h * 128:(h + 1) * 128],
                neglamT[pr][:, :], start=True, stop=True), inc=(s_pe if h == H - 1 else None))
        pe1 = s_pe.n
        em.wait("vector", s_pe, pe1)
        need("vector", "rscal0", "rscal1")
        em.op("vector", lambda e, BIG=BIG, pr=pr: e.tensor_tensor(rfs[:, :], BIG[:, :], C["ltri_sneg"][:, :], ALU.add))
        dve1 = s_dve.n
        em.wait("scalar", s_dve, dve1)
        for h in range(H):
            em.op("scalar", lambda e, h=h, pr=pr: e.activation(
                hsl(rfull[pr], h), hsl(rfs, h), AF.Exp, bias=labuf[pr][:, 16 + h:17 + h]))
        mark(f"rf{pr}", s_act)
        em.wait("tensor", s_dve, dve1)
        for h in range(H):
            em.op("tensor", lambda e, h=h, BIG=BIG, pr=pr: e.matmul(
                hsl(BIG, h), C["sel16"][:, (8 + h) * 128:(9 + h) * 128],
                lamT[pr][:, :], start=True, stop=True), inc=(s_pe if h == H - 1 else None))
        pe2 = s_pe.n
        em.wait("vector", s_pe, pe2)
        em.wait("vector", s_act)
        em.op("vector", lambda e, BIG=BIG, pr=pr: e.tensor_tensor(rfs[:, :], BIG[:, :], C["utri_incl"][:, :], ALU.add))
        dve2 = s_dve.n
        mark(f"BIGfree{pr}", s_dve)
        em.wait("scalar", s_dve, dve2)
        for h in range(H):
            em.op("scalar", lambda e, h=h, pr=pr: e.activation(
                hsl(rfullT[pr], h), hsl(rfs, h), AF.Exp, bias=mbneg[pr][:, h:h + 1]))
        mark(f"rfT{pr}", s_act)
        mark(f"rscal{pr}", s_act)

    def st_NAT(c):
        pr = c % 2
        BIG = psA if pr == 0 else psB
        ci = c % 4
        kch = lambda h: k_tg[:, h * TGT + ci * 128: h * TGT + (ci + 1) * 128]
        qch = lambda h: q_tg[:, h * TGT + ci * 128: h * TGT + (ci + 1) * 128]
        need("tensor", f"BIGfree{pr}", "qk")
        for h in range(H):
            em.op("tensor", lambda e, h=h, BIG=BIG, kch=kch: e.matmul(
                hsl(BIG, h), kch(h), kch(h), start=True, stop=True),
                inc=(s_pe if h == H - 1 else None))
        pe1 = s_pe.n
        em.wait("vector", s_pe, pe1)
        need("vector", f"rf{pr}", f"rfT{pr}")
        em.op("vector", lambda e, BIG=BIG, pr=pr: e.tensor_scalar_mul(w_tm[pr][:, :], BIG[:, :], -1.0))
        em.op("vector", lambda e, pr=pr: e.tensor_tensor(nsl[pr][:, :], rfull[pr][:, :], w_tm[pr][:, :], ALU.mult))
        em.op("vector", lambda e, pr=pr: e.tensor_tensor(u_tm[pr][:, :], rfullT[pr][:, :], C["utri_sneg"][:, :], ALU.mult))
        em.op("vector", lambda e, BIG=BIG, pr=pr: e.tensor_tensor(ntsl[pr][:, :], u_tm[pr][:, :], BIG[:, :], ALU.mult))
        dve1 = s_dve.n
        em.wait("tensor", s_dve, dve1)
        for h in range(H):
            em.op("tensor", lambda e, h=h, BIG=BIG, kch=kch, qch=qch: e.matmul(
                hsl(BIG, h), kch(h), qch(h), start=True, stop=True),
                inc=(s_pe if h == H - 1 else None))
        pe2 = s_pe.n
        em.wait("vector", s_pe, pe2)
        em.op("vector", lambda e, BIG=BIG, pr=pr: e.tensor_tensor(atsl[pr][:, :], rfullT[pr][:, :], BIG[:, :], ALU.mult))
        mark(f"AT{pr}", s_dve)
        mark(f"BIGfree{pr}", s_dve)
        em.op("vector", lambda e, pr=pr: e.tensor_tensor(chain[pr][1][:, :], ntsl[pr][:, :], C["identb_rep"][:, :], ALU.add))
        mark(f"NAT{pr}", s_dve)

    chain_state = {}

    def st_chain_k(c, k):
        pr = c % 2
        BIG = psA if pr == 0 else psB
        if k == 1:
            chain_state[pr] = dict(pcur=nsl[pr], mcur=ntsl[pr], tcur=chain[pr][1])
        cs = chain_state[pr]
        pcur, mcur, tcur = cs["pcur"], cs["mcur"], cs["tcur"]
        AUX = psE if pr == 0 else psF
        need("tensor", f"BIGfree{pr}", f"NAT{pr}", f"chev{pr}", f"AUXfree{pr}")
        em.wait("tensor", s_act)
        em.wait("tensor", s_dve)
        for h in range(H):
            em.op("tensor", lambda e, h=h, BIG=BIG, mcur=mcur, pcur=pcur: e.matmul(
                hsl(BIG, h), hsl(mcur, h), hsl(pcur, h), start=True, stop=True),
                inc=(s_pe if h == H - 1 else None))
        pe_p = s_pe.n
        if k < 6:
            for h in range(4):
                em.op("tensor", lambda e, h=h, AUX=AUX, mcur=mcur, pcur=pcur: e.matmul(
                    AUX[:, h * 128:(h + 1) * 128], hsl(pcur, h), hsl(mcur, h),
                    start=True, stop=True), inc=(s_pe if h == 3 else None))
            pe_ma = s_pe.n
            em.wait("scalar", s_pe, pe_ma)
            em.op("scalar", lambda e, AUX=AUX, mcur=mcur: e.activation(mcur[:, 0:512], AUX[:, :], AF.Identity))
            act_ma = s_act.n
            em.wait("tensor", s_act, act_ma)
            for h in range(4, H):
                em.op("tensor", lambda e, h=h, AUX=AUX, mcur=mcur, pcur=pcur: e.matmul(
                    AUX[:, (h - 4) * 128:(h - 3) * 128], hsl(pcur, h), hsl(mcur, h),
                    start=True, stop=True), inc=(s_pe if h == H - 1 else None))
            pe_mb = s_pe.n
            em.wait("scalar", s_pe, pe_mb)
            em.op("scalar", lambda e, AUX=AUX, mcur=mcur: e.activation(mcur[:, 512:1024], AUX[:, :], AF.Identity))
            mark(f"AUXfree{pr}", s_act)
            last_pread = pe_mb
        else:
            last_pread = pe_p
        em.wait("vector", s_pe, last_pread)
        need("vector", f"obwar{pr}")
        em.op("vector", lambda e, BIG=BIG, pcur=pcur: e.tensor_copy(pcur[:, :], BIG[:, :]))
        dve_p = s_dve.n
        mark(f"BIGfree{pr}", s_dve)
        need("tensor", f"BIGfree{pr}")
        em.wait("tensor", s_dve, dve_p)
        for h in range(H):
            em.op("tensor", lambda e, h=h, BIG=BIG, pcur=pcur, tcur=tcur: e.matmul(
                hsl(BIG, h), hsl(pcur, h), hsl(tcur, h), start=True, stop=True),
                inc=(s_pe if h == H - 1 else None))
        pe_a = s_pe.n
        em.wait("vector", s_pe, pe_a)
        em.op("vector", lambda e, BIG=BIG, tcur=tcur: e.tensor_tensor(
            tcur[:, :], tcur[:, :], BIG[:, :], ALU.add))
        mark(f"chev{pr}", s_dve)
        mark(f"BIGfree{pr}", s_dve)

    def st_WU(c):
        pr = c % 2
        BIG = psA if pr == 0 else psB
        tcur = chain_state[pr]["tcur"]
        need("tensor", f"BIGfree{pr}", f"ONEfree{pr}", f"gkkp{pr}", f"v{pr}")
        em.wait("tensor", s_dve)
        em.wait("tensor", s_act)
        for h in range(H):
            em.op("tensor", lambda e, h=h, BIG=BIG, tcur=tcur, pr=pr: e.matmul(
                hsl(BIG, h), hsl(tcur, h), hsl(gk_tm[pr], h), start=True, stop=True),
                inc=(s_pe if h == H - 1 else None))
        pe_w = s_pe.n
        em.wait("vector", s_pe, pe_w)
        need("vector", f"wuwar{pr}")
        em.op("vector", lambda e, BIG=BIG, pr=pr: e.tensor_copy(w_tm[pr][:, :], BIG[:, :]))
        dve_w = s_dve.n
        em.wait("tensor", s_dve, dve_w)
        for h in range(H):
            em.op("tensor", lambda e, h=h, BIG=BIG, tcur=tcur, pr=pr: e.matmul(
                hsl(BIG, h), hsl(tcur, h), hsl(v_tm[pr], h), start=True, stop=True),
                inc=(s_pe if h == H - 1 else None))
        pe_u = s_pe.n
        em.wait("scalar", s_pe, pe_u)
        em.op("scalar", lambda e, BIG=BIG, pr=pr: e.activation(u_tm[pr][:, :], BIG[:, :], AF.Identity))
        mark(f"wu{pr}", s_act)
        mark(f"BIGfree{pr}", s_act)

    def st_obase(c):
        pr = c % 2
        BIG = psA if pr == 0 else psB
        need("tensor", f"BIGfree{pr}", f"wu{pr}", f"AT{pr}")
        for h in range(H):
            em.op("tensor", lambda e, h=h, BIG=BIG, pr=pr: e.matmul(
                hsl(BIG, h), hsl(u_tm[pr], h), hsl(atsl[pr], h), start=True, stop=True),
                inc=(s_pe if h == H - 1 else None))
        pe1 = s_pe.n
        em.wait("vector", s_pe, pe1)
        need("vector", f"obwar{pr}")
        em.op("vector", lambda e, BIG=BIG, pr=pr: e.tensor_copy(chain[pr][0][:, :], BIG[:, :]))
        mark(f"BIGfree{pr}", s_dve)
        mark(f"ob{pr}", s_dve)
        need("sync", f"ob{pr}")
        em.op("sync", lambda e, c=c, pr=pr: e.dma_start(out=obase_hbm[c, :, :], in_=chain[pr][0][:, :]), inc=s_st, inck=16)
        mark(f"obwar{pr}", s_st)

    def st_zt(c):
        pr = c % 2
        BIG = psA if pr == 0 else psB
        ci = c % 4
        qch = lambda h: q_tg[:, h * TGT + ci * 128: h * TGT + (ci + 1) * 128]
        need("tensor", f"BIGfree{pr}", f"la{pr}")
        for h in range(H):
            em.op("tensor", lambda e, h=h, BIG=BIG, pr=pr: e.matmul(
                hsl(BIG, h), C["sel16"][:, (8 + h) * 128:(9 + h) * 128],
                explamT[pr][:, :], start=True, stop=True), inc=(s_pe if h == H - 1 else None))
        pe1 = s_pe.n
        em.wait("vector", s_pe, pe1)
        need("vector", f"ztwar{pr}")
        for h in range(H):
            em.op("vector", lambda e, h=h, BIG=BIG, pr=pr, qch=qch: e.tensor_tensor(
                hsl(chain[pr][1], h), qch(h), hsl(BIG, h), ALU.mult))
        dve1 = s_dve.n
        em.wait("tensor", s_dve, dve1)
        for h in range(H):
            em.op("tensor", lambda e, h=h, BIG=BIG, pr=pr: e.matmul(
                hsl(BIG, h), hsl(w_tm[pr], h), hsl(atsl[pr], h), start=True, stop=True),
                inc=(s_pe if h == H - 1 else None))
        pe2 = s_pe.n
        em.wait("vector", s_pe, pe2)
        em.op("vector", lambda e, BIG=BIG, pr=pr: e.tensor_tensor(
            zt_sl[pr][:, :], chain[pr][1][:, :], BIG[:, :], ALU.subtract))
        mark(f"BIGfree{pr}", s_dve)
        mark(f"zt{pr}", s_dve)
        need("sync", f"zt{pr}")
        em.op("sync", lambda e, c=c, pr=pr: e.dma_start(out=zt_hbm[c, :, :], in_=zt_sl[pr][:, :]), inc=s_st, inck=16)
        mark(f"ztwar{pr}", s_st)

    def st_ptq(c):
        pr = c % 2
        BIG = psA if pr == 0 else psB
        need("tensor", f"BIGfree{pr}", f"gkkp{pr}", f"wu{pr}")
        for h in range(H):
            em.op("tensor", lambda e, h=h, BIG=BIG, pr=pr: e.matmul(
                hsl(BIG, h), hsl(w_tm[pr], h), hsl(kp_tm[pr], h), start=True, stop=True),
                inc=(s_pe if h == H - 1 else None))
        pe1 = s_pe.n
        em.wait("vector", s_pe, pe1)
        need("vector", f"ptwar{pr}")
        for h in range(H):
            em.op("vector", lambda e, h=h, pr=pr: e.tensor_scalar(
                hsl(rfull[pr], h), C["identb_rep"][:, h * 128:(h + 1) * 128],
                gccol[pr][:, h:h + 1], None, ALU.mult))
        em.op("vector", lambda e, BIG=BIG, pr=pr: e.tensor_tensor(
            pt_sl[pr][:, :], rfull[pr][:, :], BIG[:, :], ALU.subtract))
        mark(f"pt{pr}", s_dve)
        mark(f"BIGfree{pr}", s_dve)
        need("sync", f"pt{pr}")
        em.op("sync", lambda e, c=c, pr=pr: e.dma_start(out=pt_hbm[c, :, :], in_=pt_sl[pr][:, :]), inc=s_st, inck=16)
        mark(f"ptwar{pr}", s_st)
        need("tensor", f"BIGfree{pr}")
        for h in range(H):
            em.op("tensor", lambda e, h=h, BIG=BIG, pr=pr: e.matmul(
                hsl(BIG, h), hsl(kp_tm[pr], h), hsl(u_tm[pr], h), start=True, stop=True),
                inc=(s_pe if h == H - 1 else None))
        pe2 = s_pe.n
        em.wait("scalar", s_pe, pe2)
        need("scalar", f"qawar{pr}")
        em.op("scalar", lambda e, BIG=BIG, pr=pr: e.activation(qa_sl[pr][:, :], BIG[:, :], AF.Identity))
        mark(f"qa{pr}", s_act)
        mark(f"BIGfree{pr}", s_act)
        need("sync", f"qa{pr}")
        em.op("sync", lambda e, c=c, pr=pr: e.dma_start(out=qa_hbm[c, :, :], in_=qa_sl[pr][:, :]), inc=s_st, inck=16)
        mark(f"qawar{pr}", s_st)

    def st_compose(c):
        pr = c % 2
        BIG = psA if pr == 0 else psB
        need("tensor", f"BIGfree{pr}", f"pt{pr}", f"qa{pr}", "psqs")
        if c == 0:
            for h in range(H):
                em.op("tensor", lambda e, h=h, BIG=BIG, pr=pr: e.matmul(
                    hsl(BIG, h), hsl(pt_sl[pr], h), C["identb"][:, :], start=True, stop=True),
                    inc=(s_pe if h == H - 1 else None))
            pe1 = s_pe.n
            em.wait("vector", s_pe, pe1)
            em.op("vector", lambda e, BIG=BIG: e.tensor_copy(ps_slab[:, :], BIG[:, :]))
            em.op("vector", lambda e, pr=pr: e.tensor_copy(qs_slab[:, :], qa_sl[pr][:, :]))
            mark("psqs", s_dve)
            mark(f"BIGfree{pr}", s_dve)
        else:
            for h in range(H):
                em.op("tensor", lambda e, h=h, BIG=BIG, pr=pr: e.matmul(
                    hsl(BIG, h), hsl(pt_sl[pr], h), hsl(ps_slab, h), start=True, stop=True),
                    inc=(s_pe if h == H - 1 else None))
            pe1 = s_pe.n
            em.wait("vector", s_pe, pe1)
            em.op("vector", lambda e, BIG=BIG: e.tensor_copy(ps_slab[:, :], BIG[:, :]))
            dve1 = s_dve.n
            em.wait("tensor", s_dve, dve1)
            for h in range(H):
                em.op("tensor", lambda e, h=h, BIG=BIG, pr=pr: e.matmul(
                    hsl(BIG, h), hsl(pt_sl[pr], h), hsl(qs_slab, h), start=True, stop=True),
                    inc=(s_pe if h == H - 1 else None))
            pe2 = s_pe.n
            em.wait("vector", s_pe, pe2)
            em.op("vector", lambda e, BIG=BIG, pr=pr: e.tensor_tensor(
                qs_slab[:, :], qa_sl[pr][:, :], BIG[:, :], ALU.add))
            mark("psqs", s_dve)
            mark(f"BIGfree{pr}", s_dve)

    P2_STAGES = ([st_v, st_la, st_R, st_NAT]
                 + [(lambda cc, kk=k: st_chain_k(cc, kk)) for k in range(1, 7)]
                 + [st_WU, st_obase, st_zt, st_ptq, st_compose])

    for tg in range(NTG):
        for c in range(tg * 4, tg * 4 + 4):
            p1_chunk(c)
        if tg == 0:
            emit_packs(len(pack_jobs))
        p2_tg_proj(tg)
        for pair in range(2):
            c0 = tg * 4 + pair * 2
            for stg in P2_STAGES:
                stg(c0)
                stg(c0 + 1)
        mark("tgfree_pe", s_pe)
        mark("tgfree_dve", s_dve)
        mark("tgfree_act", s_act)
    em.flush()
    em.join(("tensor",))
    psbf2 = psA[:, 0:512].bitcast(BF16)
    for h in range(H):
        em.op("tensor", lambda e, h=h: e.transpose(
            psbf2[:, h * 128:(h + 1) * 128], ps_slab[:, h * 128:(h + 1) * 128],
            C["identb"][:, :]), inc=(s_pe if h == H - 1 else None))
    em.wait("vector", s_pe)
    em.op("vector", lambda e: e.tensor_copy(ps_slab[:, :], psbf2[:, :]))
    em.join(("sync",))
    em.op("sync", lambda e: e.dma_start(out=ag_in[:, 0:1024], in_=ps_slab[:, :]), inc=s_st, inck=16)
    em.op("sync", lambda e: e.dma_start(out=ag_in[:, 1024:2048], in_=qs_slab[:, :]), inc=s_st, inck=16)
    em.wait("gpsimd", s_st)
    em.op("gpsimd", lambda e: e.collective_compute(
        "AllGather", ALU.bypass, replica_groups=[list(range(NCORES))],
        ins=[ag_in.opt()], outs=[ag_out.opt()]), inc=s_cc, inck=1)
    em.flush()
    stA.close()

    stB = ExitStack()
    SBB = lambda name, shape, dt: stB.enter_context(nc.sbuf_tensor(name, list(shape), dt))

    stM = ExitStack()
    SBM = lambda name, shape, dt: stM.enter_context(nc.sbuf_tensor(name, list(shape), dt))
    xn_tgb = SBM("xn_tgb", [128, ND * TGT], BF16)
    acc_tg = SBM("acc_tg", [128, ND * TGT], F32)
    h_all = SBM("h_all", [128, NFT * TGT], BF16)
    htmp = SBM("htmp", [128, TGT], BF16)
    wg_sl = [SBM(f"wg_sl{i}", [128, ND * 128], BF16) for i in range(3)]
    wu_sl = [SBM(f"wu_sl{i}", [128, ND * 128], BF16) for i in range(3)]
    wd_sl = [SBM(f"wd_sl{i}", [128, ND * 128], BF16) for i in range(2)]

    psm2 = [psA[:, 0:512], psA[:, 512:1024], psB[:, 0:512], psB[:, 512:1024],
            psC[:, :], psD[:, :], psE[:, :], psF[:, :]]

    slab_last_use = {}

    def p3_tg(tg):
        need("gpsimd", "xntgb_free")
        for ci in range(4):
            c = tg * 4 + ci
            em.op("gpsimd", lambda e, c=c, ci=ci: e.dma_start(
                out=xn_tgb[:, :].rearrange("p (d f t) -> p d f t", d=ND, f=4)[:, :, ci, :],
                in_=xn_hbm[c].rearrange("p (d t) -> p d t", d=ND)), inc=s_gld, inck=16)
        gu_ready = {}
        NBUF = 3
        def issue_gu(ft):
            bi = ft % NBUF
            lu = slab_last_use.get(("gu", bi))
            if lu is not None:
                em.wait("sync", s_pe, lu)
            em.wait("sync", s_pk, min(2 * ft + 2 + 8, 184) * 16)
            em.op("sync", lambda e, ft=ft, b=wg_sl[bi]: e.dma_start(out=b[:, :], in_=wgu_t[ft, :, :]), inc=s_ld, inck=16)
            em.op("sync", lambda e, ft=ft, b=wu_sl[bi]: e.dma_start(out=b[:, :], in_=wgu_t[NFT + ft, :, :]), inc=s_ld, inck=16)
            gu_ready[ft] = s_ld.n
        for ft in range(min(NBUF, NFT)):
            issue_gu(ft)
        em.wait("tensor", s_gld)
        need("tensor", "m2psfree")
        need("vector", "hall_free")
        for ft in range(NFT):
            gbuf, ubuf = wg_sl[ft % NBUF], wu_sl[ft % NBUF]
            em.wait("tensor", s_ld, gu_ready[ft])
            em.wait("tensor", s_dve, slab_psum_free.get(ft % 2, 0))
            em.wait("tensor", s_act, slab_psum_free_a.get(ft % 2, 0))
            psG = psC[:, :] if ft % 2 == 0 else psA[:, 0:512]
            psU = psD[:, :] if ft % 2 == 0 else psA[:, 512:1024]
            psR = psE[:, :] if ft % 2 == 0 else psB[:, 0:512]
            for dt in range(ND):
                em.op("tensor", lambda e, dt=dt, psG=psG, b=gbuf: e.matmul(
                    psG, b[:, dt * 128:(dt + 1) * 128], xn_tgb[:, dt * TGT:(dt + 1) * TGT],
                    start=(dt == 0), stop=(dt == ND - 1)))
            for dt in range(ND):
                em.op("tensor", lambda e, dt=dt, psU=psU, b=ubuf: e.matmul(
                    psU, b[:, dt * 128:(dt + 1) * 128], xn_tgb[:, dt * TGT:(dt + 1) * TGT],
                    start=(dt == 0), stop=(dt == ND - 1)), inc=(s_pe if dt == ND - 1 else None))
            routed = ft >= 8
            if routed:
                ex = (ft - 8) // 2
                em.op("tensor", lambda e, ex=ex, psR=psR, tg=tg: e.matmul(
                    psR, C["sel16"][:, ex * 128:(ex + 1) * 128], combT[:, tg * TGT:(tg + 1) * TGT],
                    start=True, stop=True), inc=s_pe)
            slab_last_use[("gu", ft % NBUF)] = s_pe.n
            em.wait("scalar", s_pe)
            em.wait("scalar", s_dve)
            em.op("scalar", lambda e, psG=psG: e.activation(htmp[:, :], psG, AF.Silu))
            em.wait("vector", s_act)
            dsth = h_all[:, ft * TGT:(ft + 1) * TGT]
            em.op("vector", lambda e, psU=psU, dsth=dsth: e.tensor_tensor(dsth, htmp[:, :], psU, ALU.mult))
            if routed:
                em.op("vector", lambda e, psR=psR, dsth=dsth: e.tensor_tensor(dsth, dsth, psR, ALU.mult))
            slab_psum_free[ft % 2] = s_dve.n
            slab_psum_free_a[ft % 2] = s_act.n
            if ft + NBUF < NFT:
                issue_gu(ft + NBUF)
        mark("xntgb_free", s_pe)
        em.wait("tensor", s_dve)
        em.wait("tensor", s_act)
        wd_ready = {}
        def issue_wd(kt):
            bi = kt % 2
            lu = slab_last_use.get(("wd", bi))
            if lu is not None:
                em.wait("sync", s_pe, lu)
            em.wait("sync", s_pk, min(2 * NFT + kt + 1 + 8, 184) * 16)
            em.op("sync", lambda e, kt=kt, b=wd_sl[bi]: e.dma_start(out=b[:, :], in_=wd_t[kt, :, :]), inc=s_ld, inck=16)
            wd_ready[kt] = s_ld.n
        issue_wd(0); issue_wd(1)
        for kt in range(NFT):
            dbuf = wd_sl[kt % 2]
            em.wait("tensor", s_ld, wd_ready[kt])
            for dt in range(ND):
                em.op("tensor", lambda e, kt=kt, dt=dt, b=dbuf: e.matmul(
                    psm2[dt], b[:, dt * 128:(dt + 1) * 128], h_all[:, kt * TGT:(kt + 1) * TGT],
                    start=(kt == 0), stop=(kt == NFT - 1)),
                    inc=(s_pe if dt == ND - 1 else None))
            slab_last_use[("wd", kt % 2)] = s_pe.n
            if kt + 2 < NFT:
                issue_wd(kt + 2)
        mark("hall_free", s_pe)
        em.wait("scalar", s_pe)
        em.wait("scalar", s_st)
        for dt in range(ND):
            em.op("scalar", lambda e, dt=dt: e.activation(
                acc_tg[:, dt * TGT:(dt + 1) * TGT], psm2[dt],
                AF.Identity, bias=C["b2"][:, dt:dt + 1]), inc=(s_act if dt == ND - 1 else None))
        mark("m2psfree", s_act)
        em.wait("sync", s_act)
        em.op("sync", lambda e, tg=tg: e.dma_start(out=acc_hbm[tg, :, :], in_=acc_tg[:, :]), inc=s_st, inck=16)

    slab_psum_free = {}
    slab_psum_free_a = {}
    for tg in range(NTG):
        p3_tg(tg)
    em.join()
    em.flush()
    stM.close()

    stS = ExitStack()
    SBS = lambda name, shape, dt: stS.enter_context(nc.sbuf_tensor(name, list(shape), dt))
    o_heads = [SBS(f"o_h{h}", [128, TC], BF16) for h in range(H)]
    wo_sb = SBS("wo_sb", [128, ND * D], BF16)
    pstqs = SBS("pstqs", [128, 2 * H * 128], BF16)
    abuf = [SBS(f"abuf{i}", [128, H * 128], BF16) for i in range(2)]
    selb = SBS("selb", [128, H * 128], BF16)
    ztb = [SBS(f"ztb{i}", [128, H * 128], BF16) for i in range(2)]
    ptb = [SBS(f"ptb{i}", [128, H * 128], BF16) for i in range(2)]
    qab = [SBS(f"qab{i}", [128, H * 128], BF16) for i in range(2)]
    obb = [SBS(f"obb{i}", [128, H * 128], BF16) for i in range(2)]
    x1buf = SBS("x1buf", [128, ND * TGT], F32)
    sqbuf = SBS("sqbuf", [128, D], F32)
    sqbuf2 = SBS("sqbuf2", [128, D], F32)
    rowp2 = SBS("rowp2", [1, 512], F32)
    h2_tg = SBS("h2_tg", [128, ND * TGT], BF16)
    g1buf = SBS("g1buf", [128, (FMLP // 128) * TGT], BF16)
    acc_tg2 = SBS("acc_tg2", [128, ND * TGT], F32)
    w1_sl = [SBS(f"w1_sl{i}", [128, ND * 128], BF16) for i in range(2)]
    w2_sl = [SBS(f"w2_sl{i}", [128, ND * 128], BF16) for i in range(2)]
    rowp = SBS("rowp", [1, 512], F32)

    em.op("gpsimd", lambda e: e.dma_start(
        out=wo_sb[:, :].rearrange("p (k m) -> p k m", k=ND),
        in_=dram["wo"].rearrange("(k p) m -> p k m", p=128)), inc=s_gld, inck=16)

    em.wait("sync", s_cc)
    em.wait("vector", s_cc)
    em.wait("gpsimd", s_cc)
    em.op("vector", lambda e: e.memset(s_slab[:, :], 0.0))
    cur = 0
    for r in range(NCORES - 1):
        em.wait("sync", s_pe)
        em.wait("sync", s_dve)
        em.op("sync", lambda e, r=r: e.dma_start(out=pstqs[:, :], in_=ag_out[r * 128:(r + 1) * 128, :]),
              inc=s_ld, inck=16)
        em.wait("vector", s_ld)
        em.wait("tensor", s_ld)
        if r == 0:
            em.op("vector", lambda e: e.tensor_copy(abuf[0][:, :], pstqs[:, 1024:2048]))
        else:
            em.wait("tensor", s_dve)
            for h in range(H):
                em.op("tensor", lambda e, h=h, cur=cur: e.matmul(
                    psA[:, h * 128:(h + 1) * 128], pstqs[:, h * 128:(h + 1) * 128],
                    abuf[cur][:, h * 128:(h + 1) * 128], start=True, stop=True),
                    inc=(s_pe if h == H - 1 else None))
            em.wait("vector", s_pe)
            em.op("vector", lambda e, cur=cur: e.tensor_tensor(
                abuf[1 - cur][:, :], pstqs[:, 1024:2048], psA[:, :], ALU.add))
            cur = 1 - cur
        em.op("vector", lambda e, r=r, cur=cur: e.tensor_scalar(
            selb[:, :], abuf[cur][:, :], C["coresel"][:, r:r + 1], None, ALU.mult))
        em.op("vector", lambda e: e.tensor_tensor(s_slab[:, :], s_slab[:, :], selb[:, :], ALU.add))
    em.join()
    em.join()
    em.flush()

    seq_last_use = {}
    for c in range(NCH):
        pb = c % 2
        lu = seq_last_use.get(pb)
        if lu is not None:
            em.wait("sync", s_dve, lu)
        for nm, bufs, hbm in (("zt", ztb, zt_hbm), ("pt", ptb, pt_hbm),
                              ("qa", qab, qa_hbm), ("ob", obb, obase_hbm)):
            em.op("sync", lambda e, b=bufs[pb], hb=hbm, c=c: e.dma_start(out=b[:, :], in_=hb[c, :, :]),
                  inc=s_ld, inck=16)
        em.wait("tensor", s_ld)
        em.join(("tensor",))
        for h in range(H):
            em.op("tensor", lambda e, h=h, pb=pb: e.matmul(
                psA[:, h * 128:(h + 1) * 128], s_slab[:, h * 128:(h + 1) * 128],
                ztb[pb][:, h * 128:(h + 1) * 128], start=True, stop=True))
        for h in range(H):
            em.op("tensor", lambda e, h=h, pb=pb: e.matmul(
                psB[:, h * 128:(h + 1) * 128], ptb[pb][:, h * 128:(h + 1) * 128],
                s_slab[:, h * 128:(h + 1) * 128], start=True, stop=True),
                inc=(s_pe if h == H - 1 else None))
        em.wait("vector", s_pe)
        for h in range(H):
            em.op("vector", lambda e, h=h, pb=pb, c=c: e.tensor_tensor(
                o_heads[h][:, c * 128:(c + 1) * 128], psA[:, h * 128:(h + 1) * 128],
                obb[pb][:, h * 128:(h + 1) * 128], ALU.add))
        em.op("vector", lambda e, pb=pb: e.tensor_tensor(s_slab[:, :], psB[:, :], qab[pb][:, :], ALU.add))
        seq_last_use[pb] = s_dve.n
    em.flush()

    psm2b = [psA[:, 0:512], psA[:, 512:1024], psB[:, 0:512], psB[:, 512:1024],
             psC[:, :], psD[:, :], psE[:, :], psF[:, :]]
    mlp_last_use = {}
    sqbufs = [sqbuf, sqbuf2]
    rowps = [rowp, rowp2]
    ln2_bank_free = {}
    ln2_sq_free = {}
    ln2_rp_free = {}
    wo_ps_free = {}
    for tg in range(NTG):
        em.wait("sync", s_dve)
        em.wait("sync", s_st)
        for ci in range(4):
            c = tg * 4 + ci
            em.op("sync", lambda e, c=c, ci=ci: e.dma_start(
                out=x1buf[:, :].rearrange("p (d f t) -> p d f t", d=ND, f=4)[:, :, ci, :],
                in_=xn_hbm[c].rearrange("p (d t) -> p d t", d=ND)), inc=s_ld, inck=16)
        em.op("sync", lambda e, tg=tg: e.dma_start(out=acc_tg2[:, :], in_=acc_hbm[tg, :, :]), inc=s_ld, inck=16)
        em.wait("tensor", s_gld)
        em.wait("tensor", s_ld)
        em.wait("vector", s_ld)
        em.wait("tensor", s_dve)
        em.wait("tensor", s_act)
        for dt in range(ND):
            ps = psC if dt % 2 == 0 else psD
            if dt % 2 in wo_ps_free:
                em.wait("tensor", s_dve, wo_ps_free[dt % 2])
            for kt in range(ND):
                em.op("tensor", lambda e, dt=dt, kt=kt, ps=ps, tg=tg: e.matmul(
                    ps[:, :], wo_sb[:, kt * D + dt * 128: kt * D + (dt + 1) * 128],
                    o_heads[kt][:, tg * TGT:(tg + 1) * TGT],
                    start=(kt == 0), stop=(kt == ND - 1)), inc=(s_pe if kt == ND - 1 else None))
            em.wait("vector", s_pe)
            em.op("vector", lambda e, dt=dt, ps=ps: e.tensor_tensor(
                x1buf[:, dt * TGT:(dt + 1) * TGT], x1buf[:, dt * TGT:(dt + 1) * TGT], ps[:, :], ALU.add))
            em.op("vector", lambda e, dt=dt, ps=ps: e.tensor_tensor(
                acc_tg2[:, dt * TGT:(dt + 1) * TGT],
                acc_tg2[:, dt * TGT:(dt + 1) * TGT], ps[:, :], ALU.add), inc=s_dve)
            wo_ps_free[dt % 2] = s_dve.n
        dve_wo = s_dve.n
        for ci in range(4):
            par2 = ci % 2
            bank = psE if par2 == 0 else psF
            sqb = sqbufs[par2]
            rp = rowps[par2]
            def x1c(dt, ci=ci):
                return x1buf[:, dt * TGT + ci * 128: dt * TGT + (ci + 1) * 128]
            em.wait("vector", s_dve, dve_wo)
            if par2 in ln2_sq_free:
                em.wait("vector", s_act, ln2_sq_free[par2])
            for dt in range(ND):
                em.op("vector", lambda e, dt=dt, x1c=x1c, sqb=sqb: e.tensor_tensor(
                    sqb[:, dt * 128:(dt + 1) * 128], x1c(dt), x1c(dt), ALU.mult),
                    inc=(s_dve if dt == ND - 1 else None))
            dve_sq = s_dve.n
            em.wait("tensor", s_dve, dve_sq)
            if par2 in ln2_bank_free:
                em.wait("tensor", s_dve, ln2_bank_free[par2])
            for dt in range(ND):
                em.op("tensor", lambda e, dt=dt, x1c=x1c, bank=bank: e.matmul(
                    bank[0:1, 0:128], C["onesf_col"][:, :], x1c(dt),
                    start=(dt == 0), stop=(dt == ND - 1)))
            for dt in range(ND):
                em.op("tensor", lambda e, dt=dt, sqb=sqb, bank=bank: e.matmul(
                    bank[0:1, 128:256], C["onesf_col"][:, :], sqb[:, dt * 128:(dt + 1) * 128],
                    start=(dt == 0), stop=(dt == ND - 1)), inc=(s_pe if dt == ND - 1 else None))
            pe_sums = s_pe.n
            em.wait("vector", s_pe, pe_sums)
            if par2 in ln2_rp_free:
                em.wait("vector", s_pe, ln2_rp_free[par2])
            em.op("vector", lambda e, rp=rp, bank=bank: e.tensor_copy(rp[0:1, 0:256], bank[0:1, 0:256]))
            em.op("vector", lambda e, rp=rp: e.tensor_scalar_mul(rp[0:1, 0:128], rp[0:1, 0:128], 1.0 / D))
            em.op("vector", lambda e, rp=rp: e.tensor_scalar_mul(rp[0:1, 128:256], rp[0:1, 128:256], 1.0 / D))
            em.op("vector", lambda e, rp=rp: e.tensor_tensor(rp[0:1, 256:384], rp[0:1, 0:128], rp[0:1, 0:128], ALU.mult))
            em.op("vector", lambda e, rp=rp: e.tensor_tensor(rp[0:1, 128:256], rp[0:1, 128:256], rp[0:1, 256:384], ALU.subtract))
            em.op("vector", lambda e, rp=rp: e.tensor_scalar_add(rp[0:1, 128:256], rp[0:1, 128:256], EPS))
            em.op("vector", lambda e, rp=rp: e.reciprocal(rp[0:1, 128:256], rp[0:1, 128:256]), inc=s_dve)
            em.wait("scalar", s_dve)
            em.op("scalar", lambda e, rp=rp: e.activation(rp[0:1, 128:256], rp[0:1, 128:256], AF.Sqrt), inc=s_act)
            em.wait("tensor", s_act)
            em.op("tensor", lambda e, rp=rp, bank=bank: e.matmul(bank[:, 256:384], C["onesf_row"][:, :], rp[0:1, 0:128],
                                                                start=True, stop=True))
            em.op("tensor", lambda e, rp=rp, bank=bank: e.matmul(bank[:, 384:512], C["onesf_row"][:, :], rp[0:1, 128:256],
                                                                start=True, stop=True), inc=s_pe)
            pe_reps = s_pe.n
            ln2_rp_free[par2] = pe_reps
            em.wait("vector", s_pe, pe_reps)
            for dt in range(ND):
                em.op("vector", lambda e, dt=dt, x1c=x1c, sqb=sqb, bank=bank: e.tensor_tensor(
                    sqb[:, dt * 128:(dt + 1) * 128], x1c(dt), bank[:, 256:384], ALU.subtract))
            for dt in range(ND):
                em.op("vector", lambda e, dt=dt, sqb=sqb, bank=bank: e.tensor_tensor(
                    sqb[:, dt * 128:(dt + 1) * 128], sqb[:, dt * 128:(dt + 1) * 128],
                    bank[:, 384:512], ALU.mult), inc=(s_dve if dt == ND - 1 else None))
            ln2_bank_free[par2] = s_dve.n
            em.wait("scalar", s_dve)
            need("scalar", "h2_free")
            for dt in range(ND):
                em.op("scalar", lambda e, dt=dt, ci=ci, sqb=sqb: e.activation(
                    h2_tg[:, dt * TGT + ci * 128: dt * TGT + (ci + 1) * 128],
                    sqb[:, dt * 128:(dt + 1) * 128], AF.Identity,
                    scale=C["ln2w"][:, dt:dt + 1], bias=C["ln2b"][:, dt:dt + 1]),
                    inc=(s_act if dt == ND - 1 else None))
            ln2_sq_free[par2] = s_act.n
        em.wait("tensor", s_act)
        em.wait("tensor", s_dve)
        w1_ready = {}
        def issue_w1(ft):
            bi = ft % 2
            lu = mlp_last_use.get(("w1", bi))
            if lu is not None:
                em.wait("sync", s_pe, lu)
            em.wait("sync", s_pk, min(3 * NFT + ft + 1 + 8, 184) * 16)
            em.op("sync", lambda e, ft=ft, b=w1_sl[bi]: e.dma_start(out=b[:, :], in_=w1_t[ft, :, :]), inc=s_ld, inck=16)
            w1_ready[ft] = s_ld.n
        issue_w1(0); issue_w1(1)
        for ft in range(FMLP // 128):
            wb = w1_sl[ft % 2]
            ps = psC if ft % 2 == 0 else psD
            em.wait("tensor", s_ld, w1_ready[ft])
            em.wait("tensor", s_act, mlp_last_use.get(("psg", ft % 2), 0))
            for dt in range(ND):
                em.op("tensor", lambda e, dt=dt, ps=ps, b=wb: e.matmul(
                    ps[:, :], b[:, dt * 128:(dt + 1) * 128], h2_tg[:, dt * TGT:(dt + 1) * TGT],
                    start=(dt == 0), stop=(dt == ND - 1)), inc=(s_pe if dt == ND - 1 else None))
            mlp_last_use[("w1", ft % 2)] = s_pe.n
            em.wait("scalar", s_pe)
            em.op("scalar", lambda e, ft=ft, ps=ps: e.activation(
                g1buf[:, ft * TGT:(ft + 1) * TGT], ps[:, :], AF.Gelu,
                bias=C["b1"][:, ft:ft + 1]))
            mlp_last_use[("psg", ft % 2)] = s_act.n
            if ft + 2 < FMLP // 128:
                issue_w1(ft + 2)
        mark("h2_free", s_pe)
        em.join(("tensor",))
        w2_ready = {}
        def issue_w2(kt):
            bi = kt % 2
            lu = mlp_last_use.get(("w2", bi))
            if lu is not None:
                em.wait("sync", s_pe, lu)
            em.wait("sync", s_pk, min(3 * NFT + FMLP // 128 + kt + 1 + 8, 184) * 16)
            em.op("sync", lambda e, kt=kt, b=w2_sl[bi]: e.dma_start(out=b[:, :], in_=w2_t[kt, :, :]), inc=s_ld, inck=16)
            w2_ready[kt] = s_ld.n
        issue_w2(0); issue_w2(1)
        for kt in range(FMLP // 128):
            wb = w2_sl[kt % 2]
            em.wait("tensor", s_ld, w2_ready[kt])
            for dt in range(ND):
                em.op("tensor", lambda e, kt=kt, dt=dt, b=wb: e.matmul(
                    psm2b[dt], b[:, dt * 128:(dt + 1) * 128], g1buf[:, kt * TGT:(kt + 1) * TGT],
                    start=(kt == 0), stop=(kt == FMLP // 128 - 1)),
                    inc=(s_pe if dt == ND - 1 else None))
            mlp_last_use[("w2", kt % 2)] = s_pe.n
            if kt + 2 < FMLP // 128:
                issue_w2(kt + 2)
        em.wait("vector", s_pe)
        for dt in range(ND):
            em.op("vector", lambda e, dt=dt: e.tensor_tensor(
                acc_tg2[:, dt * TGT:(dt + 1) * TGT],
                acc_tg2[:, dt * TGT:(dt + 1) * TGT], psm2b[dt], ALU.add))
        em.join(("sync",))
        em.op("sync", lambda e, tg=tg: e.dma_start(out=acc_hbm[tg, :, :], in_=acc_tg2[:, :]), inc=s_st, inck=16)
        em.join()
        em.flush()
    stS.close()

    stF = ExitStack()
    SBF = lambda name, shape, dt: stF.enter_context(nc.sbuf_tensor(name, list(shape), dt))
    accf = SBF("accf", [128, D], F32)
    sumf = SBF("sumf", [128, D], F32)
    outtm = SBF("outtm", [128, D], F32)
    acc_tg3 = SBF("acc_tg3", [128, ND * TGT], F32)
    for tg in range(NTG):
        em.wait("sync", s_dve)
        em.op("sync", lambda e, tg=tg: e.dma_start(out=acc_tg3[:, :], in_=acc_hbm[tg, :, :]), inc=s_ld, inck=16)
        for ci in range(4):
            c = tg * 4 + ci
            em.wait("sync", s_dve)
            em.op("sync", lambda e, c=c: e.dma_start(out=accf[:, :], in_=base_hbm[c, :, :]), inc=s_ld, inck=16)
            em.wait("vector", s_ld)
            em.op("vector", lambda e, ci=ci: e.tensor_tensor(
                sumf[:, :].rearrange("p (d t) -> p d t", d=ND),
                acc_tg3[:, :].rearrange("p (d t) -> p d t", d=ND)[:, :, ci * 128:(ci + 1) * 128],
                accf[:, :].rearrange("p (d t) -> p d t", d=ND), ALU.add), inc=s_dve)
            em.wait("tensor", s_dve)
            for dt in range(ND):
                em.op("tensor", lambda e, dt=dt: e.transpose(
                    psA[:, dt * 128:(dt + 1) * 128], sumf[:, dt * 128:(dt + 1) * 128], C["identf"][:, :]),
                    inc=(s_pe if dt == ND - 1 else None))
            em.wait("vector", s_pe)
            em.op("vector", lambda e: e.tensor_copy(outtm[:, :], psA[:, :]), inc=s_dve)
            em.wait("sync", s_dve)
            em.op("sync", lambda e, c=c: e.dma_start(out=y[c * 128:(c + 1) * 128, :], in_=outtm[:, :]), inc=s_st, inck=16)
    em.flush()
    stF.close()

    em.wait("sync", s_st)
    em.flush()
    stB.close()
    st.close()
    return nc


def _build_consts():
    ar = np.arange(128)
    consts = {}
    consts["identf"] = np.eye(128, dtype=np.float32)
    consts["identb"] = np.eye(128, dtype=np.float32)
    consts["utri"] = (ar[:, None] <= ar[None, :]).astype(np.float32)
    lr = np.zeros((128, 128), np.float32); lr[127, :] = 1.0
    consts["lastrow"] = lr
    sel = np.zeros((16, 16 * 128), np.float32)
    for j in range(16):
        sel[j, j * 128:(j + 1) * 128] = 1.0
    consts["sel16"] = sel
    lt = (ar[None, :] < ar[:, None]).astype(np.float32)
    ut = (ar[:, None] < ar[None, :]).astype(np.float32)
    ui = (ar[:, None] <= ar[None, :]).astype(np.float32)
    consts["ltri_sneg"] = np.tile(-1000.0 * (1.0 - lt), (1, H))
    consts["utri_sneg"] = np.tile(-ut, (1, H))
    consts["utri_incl"] = np.tile(-1000.0 * (1.0 - ui), (1, H))
    consts["onesf_row"] = np.ones((1, 128), np.float32)
    consts["onesf_col"] = np.ones((128, 1), np.float32)
    return consts


_NC_CACHE = {}


def kernel(L0_attn_x, L0_attn_Wq, L0_attn_Wk, L0_attn_Wv, L0_attn_Wb, L0_attn_Wa,
           L0_attn_Wo, L0_attn_ln1_w, L0_attn_ln1_b, L0_attn_ln2_w, L0_attn_ln2_b,
           L0_attn_W1, L0_attn_b1, L0_attn_W2, L0_attn_b2, L0_ffn_router_w,
           L0_ffn_wg_r, L0_ffn_wu_r, L0_ffn_wd_r, L0_ffn_wg_s, L0_ffn_wu_s,
           L0_ffn_wd_s):
    import ml_dtypes

    x = np.ascontiguousarray(np.asarray(L0_attn_x, np.float32).reshape(16384, 1024))
    f32 = lambda a: np.ascontiguousarray(np.asarray(a, np.float32))

    def colmaj(vec):
        return np.ascontiguousarray(f32(vec).reshape(-1, 128).T)

    consts = _build_consts()
    wab = np.ascontiguousarray(np.concatenate([f32(L0_attn_Wa), f32(L0_attn_Wb)], axis=1))
    wg_s = f32(L0_ffn_wg_s).transpose(1, 0, 2).reshape(1024, NSH * MIDS)
    wg_r = f32(L0_ffn_wg_r).transpose(1, 0, 2).reshape(1024, NE * MIDR)
    wu_s = f32(L0_ffn_wu_s).transpose(1, 0, 2).reshape(1024, NSH * MIDS)
    wu_r = f32(L0_ffn_wu_r).transpose(1, 0, 2).reshape(1024, NE * MIDR)
    wgu = np.ascontiguousarray(np.concatenate([wg_s, wg_r, wu_s, wu_r], axis=1))
    wd = np.ascontiguousarray(np.concatenate(
        [f32(L0_ffn_wd_s).reshape(NSH * MIDS, 1024), f32(L0_ffn_wd_r).reshape(NE * MIDR, 1024)], axis=0))

    if "nc" not in _NC_CACHE:
        _NC_CACHE["nc"] = build_nc()
    nc = _NC_CACHE["nc"]

    in_maps = []
    for core in range(NCORES):
        cs = np.zeros((128, NCORES), np.float32)
        if core > 0:
            cs[:, core - 1] = 1.0
        m = {
            "x": x[core * TC:(core + 1) * TC],
            "wq": f32(L0_attn_Wq), "wk": f32(L0_attn_Wk), "wv": f32(L0_attn_Wv),
            "wab": wab, "wo": f32(L0_attn_Wo),
            "ln1w": colmaj(L0_attn_ln1_w), "ln1b": colmaj(L0_attn_ln1_b),
            "ln2w": colmaj(L0_attn_ln2_w), "ln2b": colmaj(L0_attn_ln2_b),
            "w1": f32(L0_attn_W1), "b1": colmaj(L0_attn_b1),
            "w2": f32(L0_attn_W2), "b2": colmaj(L0_attn_b2),
            "rw": f32(L0_ffn_router_w), "wgu": wgu, "wd": wd,
            "identb": consts["identb"].astype(ml_dtypes.bfloat16),
            "identb_rep": np.tile(consts["identb"], (1, H)).astype(ml_dtypes.bfloat16),
            "coresel": cs,
        }
        for nm in ("identf", "utri", "lastrow", "sel16", "ltri_sneg",
                   "utri_sneg", "utri_incl", "onesf_row", "onesf_col"):
            m[nm] = consts[nm]
        in_maps.append(m)

    res = run_bass_kernel_spmd(nc, in_maps, core_ids=list(range(NCORES)))

    out = np.concatenate([np.asarray(res.results[c]["y"]) for c in range(NCORES)], axis=0)
    return out.reshape(1, 16384, 1024).astype(np.float32)
